# version 15
# baseline (speedup 1.0000x reference)
import numpy as np
from contextlib import ExitStack
import concourse.bass as bass
import concourse.mybir as mybir
from concourse.bass_utils import run_bass_kernel_spmd

F32 = mybir.dt.float32
BF16 = mybir.dt.bfloat16
I32 = mybir.dt.int32
AF = mybir.ActivationFunctionType
ALU = mybir.AluOpType

H = 4096
T = 2048
NI = 32
NTG = 4
TGW = 512
SLABW = 256
STGW = 512
ICB = STGW // SLABW
HEADS = 32
SCALE = 1.0 / float(np.sqrt(128.0))
CSH = 10.0
EPS = 1e-5
KVG = 16 * 128 * T
VOFF = KVG
PHASES = 9
SCRATCH_ON = True
DEV_SLABS = 16
DEV_TAP = None

DMA_SEMS = (["cl", "lb"] + [f"xl{i}" for i in range(3)]
            + [f"wg{i}" for i in range(2)] + [f"wsl{i}" for i in range(2)]
            + [f"wscr{i}" for i in range(2)] + [f"so{i}" for i in range(2)]
            + [f"sk{i}" for i in range(2)] + [f"sv{i}" for i in range(2)]
            + [f"vb{i}" for i in range(2)] + [f"qt{i}" for i in range(2)]
            + [f"bc{i}" for i in range(2)])
ENG_SEMS = ("wsc", "wc", "wf", "mm", "ea", "ev", "lv", "lsq", "lp", "lr",
            "la", "lr2", "lap", "ps", "ax", "pv", "rc", "cn", "cc", "ini")


def _build():
    nc = bass.Bass("TRN2", target_bir_lowering=False)

    xT = nc.dram_tensor("xT", [H, T], F32, kind="ExternalInput")
    wqkv = nc.dram_tensor("wqkv", [H, 3 * H], F32, kind="ExternalInput")
    wo = nc.dram_tensor("wo", [H, H], F32, kind="ExternalInput")
    qwT = nc.dram_tensor("qwT", [H, H], I32, kind="ExternalInput")
    qkvb_cols = nc.dram_tensor("qkvb_cols", [128, 96], F32, kind="ExternalInput")
    ob_cols = nc.dram_tensor("ob_cols", [128, 32], F32, kind="ExternalInput")
    fb_cols = nc.dram_tensor("fb_cols", [128, 32], F32, kind="ExternalInput")
    sczp_cols = nc.dram_tensor("sczp_cols", [128, 2], F32, kind="ExternalInput")
    vb_row = nc.dram_tensor("vb_row", [1, H], F32, kind="ExternalInput")
    out = nc.dram_tensor("out", [H, T], F32, kind="ExternalOutput")

    r2d = nc.dram_tensor("r2d", [H, T], F32)
    r3d = nc.dram_tensor("r3d", [H, T], F32)
    wffd = nc.dram_tensor("wffd", [H, H], BF16)
    qscr = nc.dram_tensor("qscr", [HEADS * 128 * T], BF16)
    kvm = [nc.dram_tensor(f"kvm{g}", [2 * KVG], BF16) for g in range(2)]
    kva = [nc.dram_tensor(f"kva{g}", [2, 2 * KVG], BF16) for g in range(2)]
    lnd = nc.dram_tensor("lnd", [4, TGW], BF16)
    rcpd = nc.dram_tensor("rcpd", [2, TGW], F32)

    with ExitStack() as stk:
        E = stk.enter_context
        arena = E(nc.sbuf_tensor("arena", [128, NI * T], BF16))
        wslab = [E(nc.sbuf_tensor(f"wslab{i}", [128, NI * SLABW], BF16)) for i in range(2)]
        wstg_f = [E(nc.sbuf_tensor(f"wstgf{i}", [128, STGW], F32)) for i in range(2)]
        wstg_i = [w[:, :].bitcast(I32) for w in wstg_f]
        xstg = [E(nc.sbuf_tensor(f"xstg{i}", [128, TGW], F32)) for i in range(3)]
        r2b = [E(nc.sbuf_tensor(f"r2b{i}", [128, TGW], F32)) for i in range(4)]
        qkb = [E(nc.sbuf_tensor(f"qkb{i}", [128, TGW], BF16)) for i in range(2)]
        vev = [E(nc.sbuf_tensor(f"vev{i}", [128, SLABW], BF16)) for i in range(2)]
        vbc = [E(nc.sbuf_tensor(f"vbc{i}", [128, SLABW], F32)) for i in range(2)]
        sqb = [E(nc.sbuf_tensor(f"sqb{i}", [128, TGW], BF16)) for i in range(4)]
        abc = E(nc.sbuf_tensor("abc0", [128, TGW], BF16))
        bbc = E(nc.sbuf_tensor("bbc0", [128, TGW], BF16))
        qtb = [E(nc.sbuf_tensor(f"qtb{i}", [128, TGW], BF16)) for i in range(2)]
        prb = [E(nc.sbuf_tensor(f"prb{i}", [128, TGW], BF16)) for i in range(2)]
        rcpb = [E(nc.sbuf_tensor(f"rcpb{i}", [128, TGW], F32)) for i in range(2)]
        onesc = E(nc.sbuf_tensor("onesc", [128, 1], BF16))
        onesf = E(nc.sbuf_tensor("onesf", [128, 128], BF16))
        eps128 = E(nc.sbuf_tensor("eps128", [128, 1], F32))
        negc = E(nc.sbuf_tensor("negc", [128, 1], F32))
        epsc = E(nc.sbuf_tensor("epsc", [1, 1], F32))
        zeroc = E(nc.sbuf_tensor("zeroc", [128, 1], F32))
        qkvb_sb = E(nc.sbuf_tensor("qkvb_sb", [128, 96], F32))
        ob_sb = E(nc.sbuf_tensor("ob_sb", [128, 32], F32))
        fb_sb = E(nc.sbuf_tensor("fb_sb", [128, 32], F32))
        fbh_sb = E(nc.sbuf_tensor("fbh_sb", [128, 32], F32))
        sczp_sb = E(nc.sbuf_tensor("sczp_sb", [128, 2], F32))
        schalf = E(nc.sbuf_tensor("schalf", [128, 1], F32))

        SCR = [E(nc.psum_tensor(f"scr{i}", [128, 1024], F32)) for i in range(2)]
        P = [SCR[i // 2][:, (i % 2) * 512:(i % 2) * 512 + 512] for i in range(4)]
        PS = [E(nc.psum_tensor(f"pss{i}", [128, 512], F32)) for i in range(2)]
        PC = [E(nc.psum_tensor(f"pcc{i}", [128, 512], F32)) for i in range(2)]

        sems = {}
        for s in list(DMA_SEMS) + list(ENG_SEMS):
            sems[s] = E(nc.semaphore(s))
        block = E(nc.Block())

        def emit(ME, eng):
            ctr = dict((s, 0) for s in sems)
            lastfree = {}

            def on(name):
                return ME == name

            def W(sem, val):
                if val > 0:
                    eng.wait_ge(sems[sem], val)

            def INC(sem, n=1):
                ctr[sem] += n
                return ctr[sem]

            def mkw(rec):
                if rec is None:
                    return
                if isinstance(rec, list):
                    for r in rec:
                        mkw(r)
                    return
                if rec[1] > 0:
                    eng.wait_ge(sems[rec[0]], rec[1])

            def DMA(engine, sem, out_ap, in_ap, waits=None):
                val = INC(sem, 16)
                if on(engine):
                    mkw(waits)
                    eng.dma_start(out=out_ap, in_=in_ap).then_inc(sems[sem], 16)
                return (sem, val)

            class Ring:
                def __init__(self, n):
                    self.n = n
                    self.h = []
                def acq(self):
                    return self.h[-self.n] if len(self.h) >= self.n else None
                def rel(self, rec):
                    self.h.append(rec)

            DMA("sync", "cl", qkvb_sb[:, :], qkvb_cols[:, :])
            DMA("sync", "cl", ob_sb[:, :], ob_cols[:, :])
            DMA("sync", "cl", fb_sb[:, :], fb_cols[:, :])
            consts_w = DMA("sync", "cl", sczp_sb[:, :], sczp_cols[:, :])
            if on("vector"):
                eng.memset(onesc[:, :], 1.0)
                eng.memset(onesf[:, :], 1.0)
                eng.memset(eps128[:, :], EPS)
                eng.memset(negc[:, :], -CSH)
                eng.memset(epsc[:, :], EPS)
                eng.memset(zeroc[:, :], 0.0).then_inc(sems["ini"], 1)
                mkw(consts_w)
                eng.tensor_scalar_mul(schalf[:, :], sczp_sb[:, 0:1], 0.5)
                eng.tensor_scalar_mul(fbh_sb[:, :], fb_sb[:, :], 0.5).then_inc(sems["ini"], 1)
            INC("ini", 2)
            ini_w = ("ini", 1)
            ini2_w = ("ini", 2)

            xs_ring = Ring(3)
            r2_ring = Ring(4)
            qk_ring = Ring(2)
            vev_ring = Ring(2)
            vbc_ring = Ring(2)
            sq_ring = Ring(4)
            ab_ring = Ring(1)
            stg_ring = Ring(2)
            slab_ring = Ring(2)
            qt_ring = Ring(2)
            rcp_ring = Ring(2)

            state = dict(gslab=0, pg=0, jat=0, prj=0, xs=0, qdone={},
                         buffree={}, vbcfree={})

            def xload(src, offset):
                s = state["xs"] % 3
                state["xs"] += 1
                rec = DMA("gpsimd", f"xl{s}", xstg[s][:, :],
                          bass.AP(src, offset, [[T, 128], [1, TGW]]),
                          waits=xs_ring.acq())
                return s, rec

            def ln_phase(src, arena_free):
                so_tot = [("so0", ctr["so0"]), ("so1", ctr["so1"])]
                sk_tot = [("sk0", ctr["sk0"]), ("sk1", ctr["sk1"])]
                for tg in range(NTG):
                    SA, SB = (PS[0], PS[1]) if tg % 2 == 0 else (PC[0], PC[1])
                    ka, kb2 = ("PS0", "PS1") if tg % 2 == 0 else ("PC0", "PC1")
                    for ic in range(NI):
                        acol = ic * T + tg * TGW
                        ki, krec = xload(src, ic * 128 * T + tg * TGW)
                        sf = sq_ring.acq()
                        v = INC("lv")
                        q = INC("lsq")
                        if on("scalar"):
                            mkw(krec)
                            if tg == 0 and ic == 0:
                                mkw(arena_free)
                                mkw(ini_w)
                            eng.activation(arena[:, acol:acol + TGW], xstg[ki][:, :],
                                           AF.Identity, bias=zeroc[:, :],
                                           scale=1.0).then_inc(sems["lv"], 1)
                            mkw(sf)
                            eng.activation(sqb[(q - 1) % 4][:, :], xstg[ki][:, :], AF.Square,
                                           bias=zeroc[:, :], scale=1.0).then_inc(sems["lsq"], 1)
                        xs_ring.rel([("lv", v), ("lsq", q)])
                        p = INC("lp")
                        if on("tensor"):
                            W("lv", v)
                            W("lsq", q)
                            if ic == 0:
                                mkw(ini_w)
                                mkw(lastfree.get(ka))
                                mkw(lastfree.get(kb2))
                            eng.matmul(SA[:, :], onesf[:, :], arena[:, acol:acol + TGW],
                                       start=(ic == 0), stop=(ic == NI - 1))
                            eng.matmul(SB[:, :], onesf[:, :], sqb[(q - 1) % 4][:, :],
                                       start=(ic == 0), stop=(ic == NI - 1)).then_inc(sems["lp"], 1)
                        sq_ring.rel(("lp", p))
                    r = INC("lr")
                    if on("vector"):
                        W("lp", ctr["lp"])
                        if tg == 0:
                            mkw(so_tot)
                        eng.tensor_scalar_mul(rcpb[0][:, :], SA[:, :], 1.0 / 4096.0)
                        eng.tensor_scalar_mul(rcpb[1][:, :], SB[:, :], 1.0 / 4096.0)
                        eng.drain()
                        eng.tensor_mul(r2b[0][:, :], rcpb[0][:, :], rcpb[0][:, :])
                        eng.drain()
                        eng.tensor_sub(rcpb[1][:, :], rcpb[1][:, :],
                                       r2b[0][:, :]).then_inc(sems["lr"], 1)
                    lastfree[ka] = ("lr", r)
                    lastfree[kb2] = ("lr", r)
                    a = INC("la")
                    if on("scalar"):
                        W("lr", r)
                        eng.activation(r2b[1][:, :], rcpb[1][:, :], AF.Sqrt,
                                       bias=eps128[:, :], scale=1.0).then_inc(sems["la"], 1)
                    r2v = INC("lr2")
                    ap = INC("lap")
                    if on("vector"):
                        W("la", a)
                        eng.reciprocal(r2b[0][:, :], r2b[1][:, :])
                        eng.drain()
                        eng.tensor_copy(abc[:, :], r2b[0][:, :])
                        eng.tensor_mul(rcpb[1][:, :], rcpb[0][:, :], r2b[0][:, :])
                        eng.drain()
                        eng.tensor_scalar_mul(bbc[:, :], rcpb[1][:, :],
                                              -1.0).then_inc(sems["lr2"], 1)
                        if tg == 0:
                            mkw(sk_tot)
                        tq = [prb[0], prb[1], qkb[0], qkb[1]]
                        for ic in range(0, NI, 4):
                            cols = [(ic + i) * T + tg * TGW for i in range(4)]
                            for i in range(4):
                                eng.tensor_mul(tq[i][:, :], arena[:, cols[i]:cols[i] + TGW], abc[:, :])
                            eng.drain()
                            for i in range(4):
                                ins = eng.tensor_add(arena[:, cols[i]:cols[i] + TGW], tq[i][:, :], bbc[:, :])
                            eng.drain()
                        ins.then_inc(sems["lap"], 1)
                return ("lap", ctr["lap"])

            def slab_load(wt, ncols, c0, is_int, to_scratch):
                gs = state["gslab"]
                state["gslab"] += 1
                sbuf = wslab[gs % 2]
                fr = state["buffree"].get(gs % 2)
                for j in range(NI * SLABW // STGW):
                    sf = stg_ring.acq()
                    si = len(stg_ring.h) % 2
                    krec = DMA("sync" if si == 0 else "scalar", f"wg{si}",
                               (wstg_i if is_int else wstg_f)[si][:, :],
                               bass.AP(wt, (j * ICB * 128) * ncols + c0,
                                       [[ncols, 128], [128 * ncols, ICB], [1, SLABW]]),
                               waits=sf)
                    v = INC("wsc")
                    if on("vector"):
                        mkw(krec)
                        if j == 0:
                            mkw(fr)
                            if to_scratch is not None and SCRATCH_ON and gs >= 2:
                                mkw(("wscr%d" % (gs % 2), 16 * (gs // 2)))
                        dst = sbuf[:, j * STGW:(j + 1) * STGW]
                        if is_int:
                            ins = eng.tensor_scalar(dst, wstg_i[si][:, :],
                                                    sczp_sb[:, 1:2], schalf[:, 0:1],
                                                    ALU.subtract, ALU.mult)
                        else:
                            ins = eng.tensor_copy(dst, wstg_f[si][:, :])
                        ins.then_inc(sems["wsc"], 1)
                    stg_ring.rel(("wsc", v))
                ready = ("wsc", ctr["wsc"])
                if to_scratch is not None and SCRATCH_ON:
                    DMA("sync", f"wscr{gs % 2}",
                        bass.AP(wffd, to_scratch, [[H, 128], [128 * H, NI], [1, SLABW]]),
                        sbuf[:, :], waits=ready)
                return ready, gs

            def slab_load_bf16(wt, c0):
                gs = state["gslab"]
                state["gslab"] += 1
                sbuf = wslab[gs % 2]
                ready = DMA("sync", f"wsl{gs % 2}", sbuf[:, :],
                            bass.AP(wt, c0, [[H, 128], [128 * H, NI], [1, SLABW]]),
                            waits=[state["buffree"].get(gs % 2),
                                   ("wscr0", ctr["wscr0"]), ("wscr1", ctr["wscr1"])])
                return ready, gs

            def slab_done(gs):
                state["buffree"][gs % 2] = ("mm", ctr["mm"])

            def proj_phase(kind, wt, cbase, arena_ready, preload=None):
                nsl = DEV_SLABS if kind == "ffn1" else 16

                def load(sl):
                    c0 = cbase + sl * SLABW
                    if kind == "ffn2":
                        return slab_load_bf16(wt, c0)
                    scr = c0 if kind == "ffn1" else None
                    return slab_load(wt, H, c0, kind == "ffn1", scr)

                pend = preload[0] if preload else load(0)
                for sl in range(nsl):
                    c0 = cbase + sl * SLABW
                    if sl + 1 >= nsl:
                        nxt = None
                    elif preload and sl == 0:
                        nxt = preload[1]
                    else:
                        nxt = load(sl + 1)
                    ready, gs = pend
                    for otl in range(2):
                        oc = c0 + otl * 128
                        ot = oc // 128
                        for tg in range(NTG):
                            pg = state["pg"]
                            state["pg"] += 1
                            bank = P[pg % 4]
                            bn = f"P{pg % 4}"
                            g = INC("mm")
                            if on("tensor"):
                                if otl == 0 and tg == 0:
                                    mkw(ready)
                                    mkw(arena_ready)
                                mkw(lastfree.get(bn))
                                for ic in range(NI):
                                    ins = eng.matmul(
                                        bank[:, :],
                                        wslab[gs % 2][:, ic * SLABW + otl * 128: ic * SLABW + otl * 128 + 128],
                                        arena[:, ic * T + tg * TGW: ic * T + tg * TGW + TGW],
                                        start=(ic == 0), stop=(ic == NI - 1),
                                    )
                                ins.then_inc(sems["mm"], 1)
                            slot = state["prj"] % 4
                            state["prj"] += 1
                            if kind == "outp":
                                biascol = ob_sb[:, ot:ot + 1]
                            else:
                                biascol = fbh_sb[:, ot:ot + 1]
                            rfr = r2_ring.acq()
                            a = INC("ea")
                            if on("scalar"):
                                W("mm", g)
                                mkw(rfr)
                                eng.activation(r2b[slot][:, :], bank[:, :], AF.Identity,
                                               bias=biascol, scale=1.0).then_inc(sems["ea"], 1)
                            lastfree[bn] = ("ea", a)
                            rsrc = {"ffn1": xT, "outp": r2d, "ffn2": r3d}[kind]
                            xi, xrec = xload(rsrc, ot * 128 * T + tg * TGW)
                            v = INC("ev")
                            if on("vector"):
                                W("ea", a)
                                mkw(xrec)
                                eng.tensor_add(r2b[slot][:, :], r2b[slot][:, :],
                                               xstg[xi][:, :]).then_inc(sems["ev"], 1)
                            xs_ring.rel(("ev", v))
                            dstt = {"ffn1": r2d, "outp": r3d, "ffn2": out}[kind]
                            srec = DMA("gpsimd", f"so{slot % 2}",
                                       bass.AP(dstt, ot * 128 * T + tg * TGW, [[T, 128], [1, TGW]]),
                                       r2b[slot][:, :], waits=("ev", v))
                            r2_ring.rel(srec)
                    slab_done(gs)
                    pend = nxt

            def qk_tiles(c0, ready, gs, arena_ready):
                    for otl in range(2):
                        oc = c0 + otl * 128
                        for tg in range(NTG):
                            pg = state["pg"]
                            state["pg"] += 1
                            bank = P[pg % 4]
                            bn = f"P{pg % 4}"
                            g = INC("mm")
                            if on("tensor"):
                                if otl == 0 and tg == 0:
                                    mkw(ready)
                                    mkw(arena_ready)
                                mkw(lastfree.get(bn))
                                for ic in range(NI):
                                    ins = eng.matmul(
                                        bank[:, :],
                                        wslab[gs % 2][:, ic * SLABW + otl * 128: ic * SLABW + otl * 128 + 128],
                                        arena[:, ic * T + tg * TGW: ic * T + tg * TGW + TGW],
                                        start=(ic == 0), stop=(ic == NI - 1),
                                    )
                                ins.then_inc(sems["mm"], 1)
                            efr = qk_ring.acq()
                            slot = len(qk_ring.h) % 2
                            a = INC("ea")
                            if on("scalar"):
                                W("mm", g)
                                mkw(efr)
                                eng.activation(qkb[slot][:, :], bank[:, :], AF.Identity,
                                               bias=qkvb_sb[:, oc // 128: oc // 128 + 1],
                                               scale=1.0).then_inc(sems["ea"], 1)
                            lastfree[bn] = ("ea", a)
                            hid = (oc // 128) % 32
                            if oc < H:
                                dst = bass.AP(qscr, hid * 128 * T + tg * TGW, [[T, 128], [1, TGW]])
                            else:
                                dst = bass.AP(kvm[hid // 16], (hid % 16) * 128 * T + tg * TGW,
                                              [[T, 128], [1, TGW]])
                            srec = DMA("scalar", f"sk{slot}", dst, qkb[slot][:, :],
                                       waits=("ea", a))
                            qk_ring.rel(srec)
                            if oc < H and tg == NTG - 1:
                                state["qdone"][hid] = srec

            def v_tiles(sv, ready, gs, vrec, arena_ready):
                    grp = sv // 8
                    lastv = None
                    for ts in range(16):
                        pg = state["pg"]
                        state["pg"] += 1
                        bank = P[pg % 4]
                        bn = f"P{pg % 4}"
                        g = INC("mm")
                        if on("tensor"):
                            if ts == 0:
                                mkw(ready)
                                mkw(arena_ready)
                            mkw(lastfree.get(bn))
                            for ic in range(NI):
                                ins = eng.matmul(
                                    bank[:, :SLABW],
                                    arena[:, ic * T + ts * 128: ic * T + ts * 128 + 128],
                                    wslab[gs % 2][:, ic * SLABW: (ic + 1) * SLABW],
                                    start=(ic == 0), stop=(ic == NI - 1),
                                )
                            ins.then_inc(sems["mm"], 1)
                        vfr2 = vev_ring.acq()
                        slot = len(vev_ring.h) % 2
                        v = INC("ev")
                        if on("vector"):
                            W("mm", g)
                            mkw(vrec)
                            mkw(vfr2)
                            eng.tensor_add(vev[slot][:, :], bank[:, :SLABW],
                                           vbc[sv % 2][:, :]).then_inc(sems["ev"], 1)
                        lastfree[bn] = ("ev", v)
                        lastv = ("ev", v)
                        h0 = (sv * SLABW) // 128
                        srec = DMA("scalar", f"sv{slot}",
                                   bass.AP(kvm[grp], VOFF + (h0 % 16) * T * 128 + ts * 128 * 128,
                                           [[128, 128], [T * 128, 2], [1, 128]]),
                                   vev[slot][:, :], waits=("ev", v))
                        vev_ring.rel(srec)
                    state["vbcfree"][sv % 2] = lastv

            def store_totals():
                return [(s, ctr[s]) for s in ("so0", "so1", "sk0", "sk1", "sv0", "sv1")
                        if ctr[s] > 0]

            QKV_SLABS = ([("k", H + i * SLABW) for i in range(8)]
                         + [("v", i) for i in range(8)]
                         + [("k", H + (8 + i) * SLABW) for i in range(8)]
                         + [("v", 8 + i) for i in range(8)]
                         + [("k", i * SLABW) for i in range(16)])

            def qkv_load(idx):
                if idx >= len(QKV_SLABS):
                    return None
                kind, a = QKV_SLABS[idx]
                if kind == "v":
                    c0 = 2 * H + a * SLABW
                    r, g = slab_load(wqkv, 3 * H, c0, False, None)
                    vrec = DMA("sync", f"vb{a % 2}", vbc[a % 2][:, :],
                               bass.AP(vb_row, a * SLABW, [[0, 128], [1, SLABW]]),
                               waits=state["vbcfree"].get(a % 2))
                    return r, g, vrec
                r, g = slab_load(wqkv, 3 * H, a, False, None)
                return r, g, None

            def emit_collective(grp):
                INC("cc")
                if on("gpsimd"):
                    mkw(store_totals())
                    eng.collective_compute(
                        "AllGather", ALU.bypass,
                        replica_groups=[[0, 1], [2, 3], [4, 5], [6, 7]],
                        ins=[kvm[grp].ap().opt()], outs=[kva[grp].ap().opt()],
                    ).then_inc(sems["cc"], 1)

            def qkv_phase(arena_ready, preload):
                pend = preload[0] if preload else qkv_load(0)
                for idx in range(len(QKV_SLABS)):
                    if preload and idx == 0:
                        nxt = preload[1]
                    else:
                        nxt = qkv_load(idx + 1)
                    kind, a = QKV_SLABS[idx]
                    ready, gs, vrec = pend
                    if kind == "v":
                        v_tiles(a, ready, gs, vrec, arena_ready)
                    else:
                        qk_tiles(a, ready, gs, arena_ready)
                    slab_done(gs)
                    pend = nxt
                    if idx == 15:
                        emit_collective(0)
                    elif idx == 31:
                        emit_collective(1)

            def attention():
                kvload = {}
                qload = {}
                mm_tot = ctr["mm"]
                ps_base, ax_base, pv_base = ctr["ps"], ctr["ax"], ctr["pv"]
                start_tot = [("wsc", ctr["wsc"]), ("lv", ctr["lv"]),
                             ("lsq", ctr["lsq"]), ("ev", ctr["ev"])]
                prb4 = [wstg_f[0][:, :].bitcast(BF16), wstg_f[1][:, :].bitcast(BF16),
                        xstg[0][:, :].bitcast(BF16), xstg[1][:, :].bitcast(BF16)]
                NQG = HEADS * NTG
                GPQ = 16
                NG = NQG * GPQ

                def load_kv(h):
                    grp = h // 16
                    ws = f"wsl{h % 2}"
                    first_waits = [("cc", grp + 1)]
                    if h < 2:
                        first_waits.append(("mm", mm_tot))
                    else:
                        first_waits.append(("pv", pv_base + (h - 1) * NTG * GPQ))
                    recs = None
                    for half2 in range(2):
                        DMA("gpsimd", ws,
                            wslab[h % 2][:, half2 * T: half2 * T + T],
                            bass.AP(kva[grp], half2 * 2 * KVG + (h % 16) * 128 * T,
                                    [[T, 128], [1, T]]),
                            waits=first_waits if half2 == 0 else None)
                        recs = DMA("gpsimd", ws,
                                   wslab[h % 2][:, H + half2 * T: H + half2 * T + T],
                                   bass.AP(kva[grp], half2 * 2 * KVG + VOFF + (h % 16) * T * 128,
                                           [[128, 128], [16384, 16], [1, 128]]))
                    kvload[h] = recs

                def load_q(j):
                    if j >= NQG:
                        return
                    hq, qgq = j // NTG, j % NTG
                    w = [state["qdone"][hq]]
                    if j >= 2:
                        w.append(("ps", ps_base + (j - 1) * GPQ))
                    qload[j] = DMA("sync", f"qt{j % 2}", qtb[j % 2][:, :],
                                   bass.AP(qscr, hq * 128 * T + qgq * TGW,
                                           [[T, 128], [1, TGW]]),
                                   waits=w)

                def emit_ps(g, extraw):
                    j2, h2 = g // GPQ, g // (NTG * GPQ)
                    INC("ps")
                    if on("tensor"):
                        mkw(extraw)
                        for half in range(2):
                            kc = 2 * (g % GPQ) + half
                            ins = eng.matmul(
                                SCR[g % 2][:, half * 512:half * 512 + 512],
                                wslab[h2 % 2][:, kc * 128:(kc + 1) * 128],
                                qtb[j2 % 2][:, :], start=True, stop=True)
                        ins.then_inc(sems["ps"], 1)

                def emit_pv(g):
                    j2, kb2, h2 = g // GPQ, g % GPQ, g // (NTG * GPQ)
                    INC("pv")
                    if on("tensor"):
                        if kb2 == 0:
                            mkw([lastfree.get(f"PC{j2 % 2}"),
                                 lastfree.get(f"PS{j2 % 2}")])
                        for half in range(2):
                            kc = 2 * kb2 + half
                            st = (kb2 == 0 and half == 0)
                            sp = (kb2 == GPQ - 1 and half == 1)
                            eng.matmul(PC[j2 % 2][:, :],
                                       wslab[h2 % 2][:, H + kc * 128: H + kc * 128 + 128],
                                       prb4[g % 4][:, half * 512:half * 512 + 512],
                                       start=st, stop=sp)
                            ins = eng.matmul(PS[j2 % 2][:, :], onesf[:, :],
                                             prb4[g % 4][:, half * 512:half * 512 + 512],
                                             start=st, stop=sp)
                        ins.then_inc(sems["pv"], 1)

                def emit_ax(g):
                    INC("ax")
                    if on("scalar"):
                        w = [("ps", ps_base + g + 1)]
                        if g >= 4:
                            w.append(("pv", pv_base + g - 3))
                        else:
                            w += start_tot
                        mkw(w)
                        eng.activation(prb4[g % 4], SCR[g % 2][:, :], AF.Exp,
                                       bias=negc[:, :], scale=SCALE
                                       ).then_inc(sems["ax"], 1)

                load_q(0)
                load_q(1)
                for j in range(NQG):
                    h, qg = j // NTG, j % NTG
                    if qg == 0:
                        if h == 0:
                            load_kv(0)
                            load_kv(1)
                        elif h < HEADS - 1:
                            load_kv(h + 1)
                    for sb in range(GPQ // 2):
                        g0 = j * GPQ + 2 * sb
                        w = []
                        if sb == 0:
                            w += [qload[j], kvload[h]]
                        if g0 >= 2:
                            w.append(("ax", ax_base + g0))
                        emit_ps(g0, w)
                        emit_ps(g0 + 1, None)
                        if g0 >= 2:
                            emit_pv(g0 - 2)
                            emit_pv(g0 - 1)
                        emit_ax(g0)
                        emit_ax(g0 + 1)
                    qt_ring.rel(("ps", ps_base + (j + 1) * GPQ))
                    load_q(j + 2)
                    rf = rcp_ring.acq()
                    rv = INC("rc")
                    cv = INC("cn")
                    if on("vector"):
                        W("pv", pv_base + (j + 1) * GPQ)
                        mkw(rf)
                        eng.reciprocal(rcpb[j % 2][:, :], PS[j % 2][:, :]
                                       ).then_inc(sems["rc"], 1)
                        eng.drain()
                        eng.tensor_tensor(arena[:, h * T + qg * TGW: h * T + qg * TGW + TGW],
                                          PC[j % 2][:, :], rcpb[j % 2][:, :], ALU.mult
                                          ).then_inc(sems["cn"], 1)
                    lastfree[f"PS{j % 2}"] = ("rc", rv)
                    lastfree[f"PC{j % 2}"] = ("cn", cv)
                    rcp_ring.rel(("cn", cv))
                emit_pv(NG - 2)
                emit_pv(NG - 1)
                for i in range(4):
                    lastfree[f"P{i}"] = ("ax", ctr["ax"])
                pvtot = ("pv", ctr["pv"])
                stg_ring.rel(pvtot)
                stg_ring.rel(pvtot)
                for _ in range(3):
                    xs_ring.rel(pvtot)
                return ("cn", ctr["cn"])

            lvl = PHASES
            pre1 = [slab_load(qwT, H, 0, True, 0),
                    slab_load(qwT, H, SLABW, True, SLABW)]
            ln1 = ln_phase(xT, None)
            if lvl >= 2:
                proj_phase("ffn1", qwT, 0, ln1, preload=pre1)
                if on("gpsimd"):
                    mkw(store_totals())
            if lvl >= 3:
                preq = [qkv_load(0), qkv_load(1)]
                ln2 = ln_phase(r2d, ("mm", ctr["mm"]))
            if lvl >= 4:
                qkv_phase(ln2, preq)
            if lvl >= 6:
                attn = attention()
                state["buffree"][0] = ("pv", ctr["pv"])
                state["buffree"][1] = ("pv", ctr["pv"])
            if lvl >= 7:
                proj_phase("outp", wo, 0, attn)
                if on("gpsimd"):
                    mkw(store_totals())
            if lvl >= 8:
                pre2 = [slab_load_bf16(wffd, 0), slab_load_bf16(wffd, SLABW)]
                ln3 = ln_phase(r3d, ("mm", ctr["mm"]))
            if lvl >= 9:
                proj_phase("ffn2", wffd, 0, ln3, preload=pre2)
                if on("gpsimd"):
                    mkw(store_totals())

            if DEV_TAP is not None:
                if DEV_TAP == "arena":
                    for blk in range(NI):
                        for part in range(4):
                            v = INC("ev")
                            if on("vector"):
                                W("cl", ctr["cl"])
                                eng.tensor_copy(xstg[0][:, :],
                                                arena[:, blk * T + part * TGW: blk * T + (part + 1) * TGW]
                                                ).then_inc(sems["ev"], 1)
                            c = INC("cl", 16)
                            if on("gpsimd"):
                                W("ev", v)
                                eng.dma_start(out=bass.AP(out, blk * 128 * T + part * TGW,
                                                          [[T, 128], [1, TGW]]),
                                              in_=xstg[0][:, :]).then_inc(sems["cl"], 16)
                    if on("gpsimd"):
                        W("cl", ctr["cl"])
                else:
                    srcs = {"r2d": (r2d, [[T, H], [1, T]], 0),
                            "r3d": (r3d, [[T, H], [1, T]], 0),
                            "qscr": (qscr, [[1, HEADS * 128 * T]], 1),
                            "kvm0": (kvm[0], [[1, 2 * KVG]], 1),
                            "kvm1": (kvm[1], [[1, 2 * KVG]], 1),
                            "kva0": (kva[0], [[2 * KVG, 2], [1, 2 * KVG]], 1),
                            "kva1": (kva[1], [[2 * KVG, 2], [1, 2 * KVG]], 1),
                            "wffd": (wffd, [[H, H], [1, H]], 1)}
                    t, ap, isbf = srcs[DEV_TAP]
                    c = INC("cl", 16)
                    if on("gpsimd"):
                        mkw(store_totals())
                        if ctr["cc"] > 0:
                            W("cc", ctr["cc"])
                        src_ap = bass.AP(t, 0, ap)
                        n = 1
                        for st_, c_ in ap:
                            n *= c_
                        if isbf:
                            src_ap = src_ap.bitcast(F32)
                            n //= 2
                        eng.dma_start(out=bass.AP(out, 0, [[1, n]]), in_=src_ap
                                      ).then_inc(sems["cl"], 16)
                        W("cl", ctr["cl"])

        @block.sync
        def _(e):
            emit("sync", e)

        @block.gpsimd
        def _(e):
            emit("gpsimd", e)

        @block.vector
        def _(e):
            emit("vector", e)

        @block.scalar
        def _(e):
            emit("scalar", e)

        @block.tensor
        def _(e):
            emit("tensor", e)

    return nc


_NC_CACHE = None


def kernel(L0_attn_x, L0_attn_attn_qkvw, L0_attn_attn_qkvb, L0_attn_attn_ow,
           L0_attn_attn_ob, L0_ffn_qweight, L0_ffn_scale, L0_ffn_zero_point,
           L0_ffn_bias):
    global _NC_CACHE
    x = np.asarray(L0_attn_x, dtype=np.float32)
    wqkv = np.ascontiguousarray(np.asarray(L0_attn_attn_qkvw, dtype=np.float32))
    qkvb = np.asarray(L0_attn_attn_qkvb, dtype=np.float32)
    wo = np.ascontiguousarray(np.asarray(L0_attn_attn_ow, dtype=np.float32))
    ob = np.asarray(L0_attn_attn_ob, dtype=np.float32)
    qw = np.asarray(L0_ffn_qweight, dtype=np.int32)
    fb = np.asarray(L0_ffn_bias, dtype=np.float32)

    qwT = np.ascontiguousarray(qw.T)
    qkvb_cols = np.ascontiguousarray(qkvb.reshape(96, 128).T)
    ob_cols = np.ascontiguousarray(ob.reshape(32, 128).T)
    fb_cols = np.ascontiguousarray(fb.reshape(32, 128).T)
    sczp_cols = np.empty((128, 2), dtype=np.float32)
    sczp_cols[:, 0] = np.float32(L0_ffn_scale)
    sczp_cols[:, 1] = np.float32(L0_ffn_zero_point)
    vb_row = np.ascontiguousarray(qkvb[2 * H:3 * H].reshape(1, H))

    if _NC_CACHE is None:
        _NC_CACHE = _build()
    nc = _NC_CACHE

    in_maps = []
    for c in range(8):
        b, half = c // 2, c % 2
        xTc = np.ascontiguousarray(x[b, half * T:(half + 1) * T, :].T)
        in_maps.append({
            "xT": xTc, "wqkv": wqkv, "wo": wo, "qwT": qwT,
            "qkvb_cols": qkvb_cols, "ob_cols": ob_cols, "fb_cols": fb_cols,
            "sczp_cols": sczp_cols, "vb_row": vb_row,
        })
    res = run_bass_kernel_spmd(nc, in_maps, core_ids=list(range(8)))
    full = np.empty((4, 4096, 4096), dtype=np.float32)
    for c in range(8):
        b, half = c // 2, c % 2
        full[b, half * T:(half + 1) * T, :] = res.results[c]["out"].T
    return full


# version 18
# speedup vs baseline: 1.0360x; 1.0360x over previous
import numpy as np
from contextlib import ExitStack
import concourse.bass as bass
import concourse.mybir as mybir
from concourse.bass_utils import run_bass_kernel_spmd

F32 = mybir.dt.float32
BF16 = mybir.dt.bfloat16
I32 = mybir.dt.int32
AF = mybir.ActivationFunctionType
ALU = mybir.AluOpType

H = 4096
T = 2048
NI = 32
NTG = 4
TGW = 512
SLABW = 256
STGW = 512
ICB = STGW // SLABW
HEADS = 32
SCALE = 1.0 / float(np.sqrt(128.0))
CSH = 10.0
EPS = 1e-5
KVG = 16 * 128 * T
VOFF = KVG
PHASES = 9
SCRATCH_ON = True
DEV_SLABS = 16
DEV_TAP = None

DMA_SEMS = (["cl", "lb"] + [f"xl{i}" for i in range(4)]
            + [f"wg{i}" for i in range(2)] + [f"wsl{i}" for i in range(2)]
            + [f"wscr{i}" for i in range(2)] + [f"so{i}" for i in range(2)]
            + [f"sk{i}" for i in range(2)] + [f"sv{i}" for i in range(2)]
            + [f"vb{i}" for i in range(2)] + [f"qt{i}" for i in range(2)]
            + [f"bc{i}" for i in range(2)])
ENG_SEMS = ("wsc", "wc", "wf", "mm", "ea", "ev", "lv", "lsq", "lp", "lr",
            "la", "lr2", "lap", "ps", "ax", "pv", "rc", "cn", "cc", "ini")


def _build():
    nc = bass.Bass("TRN2", target_bir_lowering=False)

    xT = nc.dram_tensor("xT", [H, T], F32, kind="ExternalInput")
    wqkv = nc.dram_tensor("wqkv", [H, 3 * H], F32, kind="ExternalInput")
    wo = nc.dram_tensor("wo", [H, H], F32, kind="ExternalInput")
    qwT = nc.dram_tensor("qwT", [H, H], I32, kind="ExternalInput")
    qkvb_cols = nc.dram_tensor("qkvb_cols", [128, 96], F32, kind="ExternalInput")
    ob_cols = nc.dram_tensor("ob_cols", [128, 32], F32, kind="ExternalInput")
    fb_cols = nc.dram_tensor("fb_cols", [128, 32], F32, kind="ExternalInput")
    sczp_cols = nc.dram_tensor("sczp_cols", [128, 2], F32, kind="ExternalInput")
    vb_row = nc.dram_tensor("vb_row", [1, H], F32, kind="ExternalInput")
    out = nc.dram_tensor("out", [H, T], F32, kind="ExternalOutput")

    r2d = nc.dram_tensor("r2d", [H, T], F32)
    r3d = nc.dram_tensor("r3d", [H, T], F32)
    wffd = nc.dram_tensor("wffd", [H, H], BF16)
    qscr = nc.dram_tensor("qscr", [HEADS * 128 * T], BF16)
    kvm = [nc.dram_tensor(f"kvm{g}", [2 * KVG], BF16) for g in range(2)]
    kva = [nc.dram_tensor(f"kva{g}", [2, 2 * KVG], BF16) for g in range(2)]
    lnd = nc.dram_tensor("lnd", [4, TGW], BF16)
    rcpd = nc.dram_tensor("rcpd", [2, TGW], F32)

    with ExitStack() as stk:
        E = stk.enter_context
        arena = E(nc.sbuf_tensor("arena", [128, NI * T], BF16))
        wslab = [E(nc.sbuf_tensor(f"wslab{i}", [128, NI * SLABW], BF16)) for i in range(2)]
        wstg_f = [E(nc.sbuf_tensor(f"wstgf{i}", [128, STGW], F32)) for i in range(2)]
        wstg_i = [w[:, :].bitcast(I32) for w in wstg_f]
        xstg = [E(nc.sbuf_tensor(f"xstg{i}", [128, TGW], F32)) for i in range(4)]
        r2b = [E(nc.sbuf_tensor(f"r2b{i}", [128, TGW], F32)) for i in range(4)]
        qkb = [E(nc.sbuf_tensor(f"qkb{i}", [128, TGW], BF16)) for i in range(4)]
        vev = [E(nc.sbuf_tensor(f"vev{i}", [128, SLABW], BF16)) for i in range(4)]
        vbc = [E(nc.sbuf_tensor(f"vbc{i}", [128, SLABW], F32)) for i in range(2)]
        sqb = [E(nc.sbuf_tensor(f"sqb{i}", [128, TGW], BF16)) for i in range(4)]
        abc = E(nc.sbuf_tensor("abc0", [128, TGW], BF16))
        bbc = E(nc.sbuf_tensor("bbc0", [128, TGW], BF16))
        qtb = [E(nc.sbuf_tensor(f"qtb{i}", [128, TGW], BF16)) for i in range(2)]
        prb = [E(nc.sbuf_tensor(f"prb{i}", [128, TGW], BF16)) for i in range(2)]
        rcpb = [E(nc.sbuf_tensor(f"rcpb{i}", [128, TGW], F32)) for i in range(2)]
        onesc = E(nc.sbuf_tensor("onesc", [128, 1], BF16))
        onesf = E(nc.sbuf_tensor("onesf", [128, 128], BF16))
        eps128 = E(nc.sbuf_tensor("eps128", [128, 1], F32))
        negc = E(nc.sbuf_tensor("negc", [128, 1], F32))
        epsc = E(nc.sbuf_tensor("epsc", [1, 1], F32))
        zeroc = E(nc.sbuf_tensor("zeroc", [128, 1], F32))
        qkvb_sb = E(nc.sbuf_tensor("qkvb_sb", [128, 96], F32))
        ob_sb = E(nc.sbuf_tensor("ob_sb", [128, 32], F32))
        fb_sb = E(nc.sbuf_tensor("fb_sb", [128, 32], F32))
        fbh_sb = E(nc.sbuf_tensor("fbh_sb", [128, 32], F32))
        sczp_sb = E(nc.sbuf_tensor("sczp_sb", [128, 2], F32))
        schalf = E(nc.sbuf_tensor("schalf", [128, 1], F32))

        SCR = [E(nc.psum_tensor(f"scr{i}", [128, 1024], F32)) for i in range(2)]
        P = [SCR[i // 2][:, (i % 2) * 512:(i % 2) * 512 + 512] for i in range(4)]
        PS = [E(nc.psum_tensor(f"pss{i}", [128, 512], F32)) for i in range(2)]
        PC = [E(nc.psum_tensor(f"pcc{i}", [128, 512], F32)) for i in range(2)]

        sems = {}
        for s in list(DMA_SEMS) + list(ENG_SEMS):
            sems[s] = E(nc.semaphore(s))
        block = E(nc.Block())

        def emit(ME, eng):
            ctr = dict((s, 0) for s in sems)
            lastfree = {}

            def on(name):
                return ME == name

            def W(sem, val):
                if val > 0:
                    eng.wait_ge(sems[sem], val)

            def INC(sem, n=1):
                ctr[sem] += n
                return ctr[sem]

            def mkw(rec):
                if rec is None:
                    return
                if isinstance(rec, list):
                    for r in rec:
                        mkw(r)
                    return
                if rec[1] > 0:
                    eng.wait_ge(sems[rec[0]], rec[1])

            def DMA(engine, sem, out_ap, in_ap, waits=None):
                val = INC(sem, 16)
                if on(engine):
                    mkw(waits)
                    eng.dma_start(out=out_ap, in_=in_ap).then_inc(sems[sem], 16)
                return (sem, val)

            class Ring:
                def __init__(self, n):
                    self.n = n
                    self.h = []
                def acq(self):
                    return self.h[-self.n] if len(self.h) >= self.n else None
                def rel(self, rec):
                    self.h.append(rec)

            DMA("sync", "cl", qkvb_sb[:, :], qkvb_cols[:, :])
            DMA("sync", "cl", ob_sb[:, :], ob_cols[:, :])
            DMA("sync", "cl", fb_sb[:, :], fb_cols[:, :])
            consts_w = DMA("sync", "cl", sczp_sb[:, :], sczp_cols[:, :])
            if on("vector"):
                eng.memset(onesc[:, :], 1.0)
                eng.memset(onesf[:, :], 1.0)
                eng.memset(eps128[:, :], EPS)
                eng.memset(negc[:, :], -CSH)
                eng.memset(epsc[:, :], EPS)
                eng.memset(zeroc[:, :], 0.0).then_inc(sems["ini"], 1)
                mkw(consts_w)
                eng.tensor_scalar_mul(schalf[:, :], sczp_sb[:, 0:1], 0.5)
                eng.tensor_scalar_mul(fbh_sb[:, :], fb_sb[:, :], 0.5).then_inc(sems["ini"], 1)
            INC("ini", 2)
            ini_w = ("ini", 1)
            ini2_w = ("ini", 2)

            xs_ring = Ring(4)
            r2_ring = Ring(4)
            qk_ring = Ring(4)
            vev_ring = Ring(4)
            vbc_ring = Ring(2)
            sq_ring = Ring(4)
            ab_ring = Ring(1)
            stg_ring = Ring(2)
            slab_ring = Ring(2)
            qt_ring = Ring(2)
            rcp_ring = Ring(2)

            state = dict(gslab=0, pg=0, jat=0, prj=0, xs=0, qdone={},
                         buffree={}, vbcfree={})

            def xload(src, offset):
                s = state["xs"] % 4
                state["xs"] += 1
                rec = DMA("gpsimd", f"xl{s}", xstg[s][:, :],
                          bass.AP(src, offset, [[T, 128], [1, TGW]]),
                          waits=xs_ring.acq())
                return s, rec

            def ln_phase(src, arena_free):
                so_tot = [("so0", ctr["so0"]), ("so1", ctr["so1"])]
                sk_tot = [("sk0", ctr["sk0"]), ("sk1", ctr["sk1"])]
                for tg in range(NTG):
                    SA, SB = (PS[0], PS[1]) if tg % 2 == 0 else (PC[0], PC[1])
                    ka, kb2 = ("PS0", "PS1") if tg % 2 == 0 else ("PC0", "PC1")
                    for ic in range(NI):
                        acol = ic * T + tg * TGW
                        ki, krec = xload(src, ic * 128 * T + tg * TGW)
                        sf = sq_ring.acq()
                        v = INC("lv")
                        q = INC("lsq")
                        if on("scalar"):
                            mkw(krec)
                            if tg == 0 and ic == 0:
                                mkw(arena_free)
                                mkw(ini_w)
                            eng.activation(arena[:, acol:acol + TGW], xstg[ki][:, :],
                                           AF.Identity, bias=zeroc[:, :],
                                           scale=1.0).then_inc(sems["lv"], 1)
                            mkw(sf)
                            eng.activation(sqb[(q - 1) % 4][:, :], xstg[ki][:, :], AF.Square,
                                           bias=zeroc[:, :], scale=1.0).then_inc(sems["lsq"], 1)
                        xs_ring.rel([("lv", v), ("lsq", q)])
                        p = INC("lp")
                        if on("tensor"):
                            W("lv", v)
                            W("lsq", q)
                            if ic == 0:
                                mkw(ini_w)
                                mkw(lastfree.get(ka))
                                mkw(lastfree.get(kb2))
                            eng.matmul(SA[:, :], onesf[:, :], arena[:, acol:acol + TGW],
                                       start=(ic == 0), stop=(ic == NI - 1))
                            eng.matmul(SB[:, :], onesf[:, :], sqb[(q - 1) % 4][:, :],
                                       start=(ic == 0), stop=(ic == NI - 1)).then_inc(sems["lp"], 1)
                        sq_ring.rel(("lp", p))
                    r = INC("lr")
                    if on("vector"):
                        W("lp", ctr["lp"])
                        if tg == 0:
                            mkw(so_tot)
                        eng.tensor_scalar_mul(rcpb[0][:, :], SA[:, :], 1.0 / 4096.0)
                        eng.tensor_scalar_mul(rcpb[1][:, :], SB[:, :], 1.0 / 4096.0)
                        eng.drain()
                        eng.tensor_mul(r2b[0][:, :], rcpb[0][:, :], rcpb[0][:, :])
                        eng.drain()
                        eng.tensor_sub(rcpb[1][:, :], rcpb[1][:, :],
                                       r2b[0][:, :]).then_inc(sems["lr"], 1)
                    lastfree[ka] = ("lr", r)
                    lastfree[kb2] = ("lr", r)
                    a = INC("la")
                    if on("scalar"):
                        W("lr", r)
                        eng.activation(r2b[1][:, :], rcpb[1][:, :], AF.Sqrt,
                                       bias=eps128[:, :], scale=1.0).then_inc(sems["la"], 1)
                    r2v = INC("lr2")
                    ap = INC("lap")
                    if on("vector"):
                        W("la", a)
                        eng.reciprocal(r2b[0][:, :], r2b[1][:, :])
                        eng.drain()
                        eng.tensor_copy(abc[:, :], r2b[0][:, :])
                        eng.tensor_mul(rcpb[1][:, :], rcpb[0][:, :], r2b[0][:, :])
                        eng.drain()
                        eng.tensor_scalar_mul(bbc[:, :], rcpb[1][:, :],
                                              -1.0).then_inc(sems["lr2"], 1)
                        if tg == 0:
                            mkw(sk_tot)
                        tq = [prb[0], prb[1], qkb[0], qkb[1]]
                        for ic in range(0, NI, 4):
                            cols = [(ic + i) * T + tg * TGW for i in range(4)]
                            for i in range(4):
                                eng.tensor_mul(tq[i][:, :], arena[:, cols[i]:cols[i] + TGW], abc[:, :])
                            eng.drain()
                            for i in range(4):
                                ins = eng.tensor_add(arena[:, cols[i]:cols[i] + TGW], tq[i][:, :], bbc[:, :])
                            eng.drain()
                        ins.then_inc(sems["lap"], 1)
                return ("lap", ctr["lap"])

            def slab_load(wt, ncols, c0, is_int, to_scratch):
                gs = state["gslab"]
                state["gslab"] += 1
                sbuf = wslab[gs % 2]
                fr = state["buffree"].get(gs % 2)
                for j in range(NI * SLABW // STGW):
                    sf = stg_ring.acq()
                    si = len(stg_ring.h) % 2
                    krec = DMA("sync", f"wg{si}",
                               (wstg_i if is_int else wstg_f)[si][:, :],
                               bass.AP(wt, (j * ICB * 128) * ncols + c0,
                                       [[ncols, 128], [128 * ncols, ICB], [1, SLABW]]),
                               waits=sf)
                    v = INC("wsc")
                    if on("vector"):
                        mkw(krec)
                        if j == 0:
                            mkw(fr)
                            if to_scratch is not None and SCRATCH_ON and gs >= 2:
                                mkw(("wscr%d" % (gs % 2), 16 * (gs // 2)))
                        dst = sbuf[:, j * STGW:(j + 1) * STGW]
                        if is_int:
                            ins = eng.tensor_scalar(dst, wstg_i[si][:, :],
                                                    sczp_sb[:, 1:2], schalf[:, 0:1],
                                                    ALU.subtract, ALU.mult)
                        else:
                            ins = eng.tensor_copy(dst, wstg_f[si][:, :])
                        ins.then_inc(sems["wsc"], 1)
                    stg_ring.rel(("wsc", v))
                ready = ("wsc", ctr["wsc"])
                if to_scratch is not None and SCRATCH_ON:
                    DMA("sync", f"wscr{gs % 2}",
                        bass.AP(wffd, to_scratch, [[H, 128], [128 * H, NI], [1, SLABW]]),
                        sbuf[:, :], waits=ready)
                return ready, gs

            def slab_load_bf16(wt, c0):
                gs = state["gslab"]
                state["gslab"] += 1
                sbuf = wslab[gs % 2]
                ready = DMA("sync", f"wsl{gs % 2}", sbuf[:, :],
                            bass.AP(wt, c0, [[H, 128], [128 * H, NI], [1, SLABW]]),
                            waits=[state["buffree"].get(gs % 2),
                                   ("wscr0", ctr["wscr0"]), ("wscr1", ctr["wscr1"])])
                return ready, gs

            def slab_done(gs):
                state["buffree"][gs % 2] = ("mm", ctr["mm"])

            def proj_phase(kind, wt, cbase, arena_ready, preload=None):
                nsl = DEV_SLABS if kind == "ffn1" else 16

                def load(sl):
                    c0 = cbase + sl * SLABW
                    if kind == "ffn2":
                        return slab_load_bf16(wt, c0)
                    scr = c0 if kind == "ffn1" else None
                    return slab_load(wt, H, c0, kind == "ffn1", scr)

                pend = preload[0] if preload else load(0)
                for sl in range(nsl):
                    c0 = cbase + sl * SLABW
                    if sl + 1 >= nsl:
                        nxt = None
                    elif preload and sl == 0:
                        nxt = preload[1]
                    else:
                        nxt = load(sl + 1)
                    ready, gs = pend
                    for otl in range(2):
                        oc = c0 + otl * 128
                        ot = oc // 128
                        for tg in range(NTG):
                            pg = state["pg"]
                            state["pg"] += 1
                            bank = P[pg % 4]
                            bn = f"P{pg % 4}"
                            g = INC("mm")
                            if on("tensor"):
                                if otl == 0 and tg == 0:
                                    mkw(ready)
                                    mkw(arena_ready)
                                mkw(lastfree.get(bn))
                                for ic in range(NI):
                                    ins = eng.matmul(
                                        bank[:, :],
                                        wslab[gs % 2][:, ic * SLABW + otl * 128: ic * SLABW + otl * 128 + 128],
                                        arena[:, ic * T + tg * TGW: ic * T + tg * TGW + TGW],
                                        start=(ic == 0), stop=(ic == NI - 1),
                                    )
                                ins.then_inc(sems["mm"], 1)
                            slot = state["prj"] % 4
                            state["prj"] += 1
                            if kind == "outp":
                                biascol = ob_sb[:, ot:ot + 1]
                            else:
                                biascol = fbh_sb[:, ot:ot + 1]
                            rfr = r2_ring.acq()
                            a = INC("ea")
                            if on("scalar"):
                                W("mm", g)
                                mkw(rfr)
                                eng.activation(r2b[slot][:, :], bank[:, :], AF.Identity,
                                               bias=biascol, scale=1.0).then_inc(sems["ea"], 1)
                            lastfree[bn] = ("ea", a)
                            rsrc = {"ffn1": xT, "outp": r2d, "ffn2": r3d}[kind]
                            xi, xrec = xload(rsrc, ot * 128 * T + tg * TGW)
                            v = INC("ev")
                            if on("vector"):
                                W("ea", a)
                                mkw(xrec)
                                eng.tensor_add(r2b[slot][:, :], r2b[slot][:, :],
                                               xstg[xi][:, :]).then_inc(sems["ev"], 1)
                            xs_ring.rel(("ev", v))
                            dstt = {"ffn1": r2d, "outp": r3d, "ffn2": out}[kind]
                            srec = DMA("gpsimd", f"so{slot % 2}",
                                       bass.AP(dstt, ot * 128 * T + tg * TGW, [[T, 128], [1, TGW]]),
                                       r2b[slot][:, :], waits=("ev", v))
                            r2_ring.rel(srec)
                    slab_done(gs)
                    pend = nxt

            def qk_tiles(c0, ready, gs, arena_ready):
                    for otl in range(2):
                        oc = c0 + otl * 128
                        for tg in range(NTG):
                            pg = state["pg"]
                            state["pg"] += 1
                            bank = P[pg % 4]
                            bn = f"P{pg % 4}"
                            g = INC("mm")
                            if on("tensor"):
                                if otl == 0 and tg == 0:
                                    mkw(ready)
                                    mkw(arena_ready)
                                mkw(lastfree.get(bn))
                                for ic in range(NI):
                                    ins = eng.matmul(
                                        bank[:, :],
                                        wslab[gs % 2][:, ic * SLABW + otl * 128: ic * SLABW + otl * 128 + 128],
                                        arena[:, ic * T + tg * TGW: ic * T + tg * TGW + TGW],
                                        start=(ic == 0), stop=(ic == NI - 1),
                                    )
                                ins.then_inc(sems["mm"], 1)
                            efr = qk_ring.acq()
                            slot = len(qk_ring.h) % 4
                            a = INC("ea")
                            if on("scalar"):
                                W("mm", g)
                                mkw(efr)
                                eng.activation(qkb[slot][:, :], bank[:, :], AF.Identity,
                                               bias=qkvb_sb[:, oc // 128: oc // 128 + 1],
                                               scale=1.0).then_inc(sems["ea"], 1)
                            lastfree[bn] = ("ea", a)
                            hid = (oc // 128) % 32
                            if oc < H:
                                dst = bass.AP(qscr, hid * 128 * T + tg * TGW, [[T, 128], [1, TGW]])
                            else:
                                dst = bass.AP(kvm[hid // 16], (hid % 16) * 128 * T + tg * TGW,
                                              [[T, 128], [1, TGW]])
                            srec = DMA("scalar", f"sk{slot % 2}", dst, qkb[slot][:, :],
                                       waits=("ea", a))
                            qk_ring.rel(srec)
                            if oc < H and tg == NTG - 1:
                                state["qdone"][hid] = srec

            def v_tiles(sv, ready, gs, vrec, arena_ready):
                    grp = sv // 8
                    lastv = None
                    for ts in range(16):
                        pg = state["pg"]
                        state["pg"] += 1
                        bank = P[pg % 4]
                        bn = f"P{pg % 4}"
                        g = INC("mm")
                        if on("tensor"):
                            if ts == 0:
                                mkw(ready)
                                mkw(arena_ready)
                            mkw(lastfree.get(bn))
                            for ic in range(NI):
                                ins = eng.matmul(
                                    bank[:, :SLABW],
                                    arena[:, ic * T + ts * 128: ic * T + ts * 128 + 128],
                                    wslab[gs % 2][:, ic * SLABW: (ic + 1) * SLABW],
                                    start=(ic == 0), stop=(ic == NI - 1),
                                )
                            ins.then_inc(sems["mm"], 1)
                        vfr2 = vev_ring.acq()
                        slot = len(vev_ring.h) % 4
                        v = INC("ev")
                        if on("vector"):
                            W("mm", g)
                            mkw(vrec)
                            mkw(vfr2)
                            eng.tensor_add(vev[slot][:, :], bank[:, :SLABW],
                                           vbc[sv % 2][:, :]).then_inc(sems["ev"], 1)
                        lastfree[bn] = ("ev", v)
                        lastv = ("ev", v)
                        h0 = (sv * SLABW) // 128
                        srec = DMA("scalar", f"sv{slot % 2}",
                                   bass.AP(kvm[grp], VOFF + (h0 % 16) * T * 128 + ts * 128 * 128,
                                           [[128, 128], [T * 128, 2], [1, 128]]),
                                   vev[slot][:, :], waits=("ev", v))
                        vev_ring.rel(srec)
                    state["vbcfree"][sv % 2] = lastv

            def store_totals():
                return [(s, ctr[s]) for s in ("so0", "so1", "sk0", "sk1", "sv0", "sv1")
                        if ctr[s] > 0]

            QKV_SLABS = ([("k", H + i * SLABW) for i in range(8)]
                         + [("v", i) for i in range(8)]
                         + [("k", H + (8 + i) * SLABW) for i in range(8)]
                         + [("v", 8 + i) for i in range(8)]
                         + [("k", i * SLABW) for i in range(16)])

            def qkv_load(idx):
                if idx >= len(QKV_SLABS):
                    return None
                kind, a = QKV_SLABS[idx]
                if kind == "v":
                    c0 = 2 * H + a * SLABW
                    r, g = slab_load(wqkv, 3 * H, c0, False, None)
                    vrec = DMA("sync", f"vb{a % 2}", vbc[a % 2][:, :],
                               bass.AP(vb_row, a * SLABW, [[0, 128], [1, SLABW]]),
                               waits=state["vbcfree"].get(a % 2))
                    return r, g, vrec
                r, g = slab_load(wqkv, 3 * H, a, False, None)
                return r, g, None

            def emit_collective(grp):
                INC("cc")
                if on("gpsimd"):
                    mkw(store_totals())
                    eng.collective_compute(
                        "AllGather", ALU.bypass,
                        replica_groups=[[0, 1], [2, 3], [4, 5], [6, 7]],
                        ins=[kvm[grp].ap().opt()], outs=[kva[grp].ap().opt()],
                    ).then_inc(sems["cc"], 1)

            def qkv_phase(arena_ready, preload):
                pend = preload[0] if preload else qkv_load(0)
                for idx in range(len(QKV_SLABS)):
                    if preload and idx == 0:
                        nxt = preload[1]
                    else:
                        nxt = qkv_load(idx + 1)
                    kind, a = QKV_SLABS[idx]
                    ready, gs, vrec = pend
                    if kind == "v":
                        v_tiles(a, ready, gs, vrec, arena_ready)
                    else:
                        qk_tiles(a, ready, gs, arena_ready)
                    slab_done(gs)
                    pend = nxt
                    if idx == 15:
                        emit_collective(0)
                    elif idx == 31:
                        emit_collective(1)

            def attention():
                kvload = {}
                qload = {}
                mm_tot = ctr["mm"]
                ps_base, ax_base, pv_base = ctr["ps"], ctr["ax"], ctr["pv"]
                start_tot = [("wsc", ctr["wsc"]), ("lv", ctr["lv"]),
                             ("lsq", ctr["lsq"]), ("ev", ctr["ev"])]
                prb4 = [wstg_f[0][:, :].bitcast(BF16), wstg_f[1][:, :].bitcast(BF16),
                        xstg[0][:, :].bitcast(BF16), xstg[1][:, :].bitcast(BF16)]
                NQG = HEADS * NTG
                GPQ = 16
                NG = NQG * GPQ

                def load_kv(h):
                    grp = h // 16
                    ws = f"wsl{h % 2}"
                    first_waits = [("cc", grp + 1)]
                    if h < 2:
                        first_waits.append(("mm", mm_tot))
                    else:
                        first_waits.append(("pv", pv_base + (h - 1) * NTG * GPQ))
                    recs = None
                    for half2 in range(2):
                        DMA("gpsimd", ws,
                            wslab[h % 2][:, half2 * T: half2 * T + T],
                            bass.AP(kva[grp], half2 * 2 * KVG + (h % 16) * 128 * T,
                                    [[T, 128], [1, T]]),
                            waits=first_waits if half2 == 0 else None)
                        recs = DMA("gpsimd", ws,
                                   wslab[h % 2][:, H + half2 * T: H + half2 * T + T],
                                   bass.AP(kva[grp], half2 * 2 * KVG + VOFF + (h % 16) * T * 128,
                                           [[128, 128], [16384, 16], [1, 128]]))
                    kvload[h] = recs

                def load_q(j):
                    if j >= NQG:
                        return
                    hq, qgq = j // NTG, j % NTG
                    w = [state["qdone"][hq]]
                    if j >= 2:
                        w.append(("ps", ps_base + (j - 1) * GPQ))
                    qload[j] = DMA("sync", f"qt{j % 2}", qtb[j % 2][:, :],
                                   bass.AP(qscr, hq * 128 * T + qgq * TGW,
                                           [[T, 128], [1, TGW]]),
                                   waits=w)

                def emit_ps(g, extraw):
                    j2, h2 = g // GPQ, g // (NTG * GPQ)
                    INC("ps")
                    if on("tensor"):
                        mkw(extraw)
                        for half in range(2):
                            kc = 2 * (g % GPQ) + half
                            ins = eng.matmul(
                                SCR[g % 2][:, half * 512:half * 512 + 512],
                                wslab[h2 % 2][:, kc * 128:(kc + 1) * 128],
                                qtb[j2 % 2][:, :], start=True, stop=True)
                        ins.then_inc(sems["ps"], 1)

                def emit_pv(g):
                    j2, kb2, h2 = g // GPQ, g % GPQ, g // (NTG * GPQ)
                    INC("pv")
                    if on("tensor"):
                        if kb2 == 0:
                            mkw([lastfree.get(f"PC{j2 % 2}"),
                                 lastfree.get(f"PS{j2 % 2}")])
                        for half in range(2):
                            kc = 2 * kb2 + half
                            st = (kb2 == 0 and half == 0)
                            sp = (kb2 == GPQ - 1 and half == 1)
                            eng.matmul(PC[j2 % 2][:, :],
                                       wslab[h2 % 2][:, H + kc * 128: H + kc * 128 + 128],
                                       prb4[g % 4][:, half * 512:half * 512 + 512],
                                       start=st, stop=sp)
                            ins = eng.matmul(PS[j2 % 2][:, :], onesf[:, :],
                                             prb4[g % 4][:, half * 512:half * 512 + 512],
                                             start=st, stop=sp)
                        ins.then_inc(sems["pv"], 1)

                def emit_ax(g):
                    INC("ax")
                    if on("scalar"):
                        w = [("ps", ps_base + g + 1)]
                        if g >= 4:
                            w.append(("pv", pv_base + g - 3))
                        else:
                            w += start_tot
                        mkw(w)
                        eng.activation(prb4[g % 4], SCR[g % 2][:, :], AF.Exp,
                                       bias=negc[:, :], scale=SCALE
                                       ).then_inc(sems["ax"], 1)

                load_q(0)
                load_q(1)
                for j in range(NQG):
                    h, qg = j // NTG, j % NTG
                    if qg == 0:
                        if h == 0:
                            load_kv(0)
                            load_kv(1)
                        elif h < HEADS - 1:
                            load_kv(h + 1)
                    for sb in range(GPQ // 2):
                        g0 = j * GPQ + 2 * sb
                        w = []
                        if sb == 0:
                            w += [qload[j], kvload[h]]
                        if g0 >= 2:
                            w.append(("ax", ax_base + g0))
                        emit_ps(g0, w)
                        emit_ps(g0 + 1, None)
                        if g0 >= 2:
                            emit_pv(g0 - 2)
                            emit_pv(g0 - 1)
                        emit_ax(g0)
                        emit_ax(g0 + 1)
                    qt_ring.rel(("ps", ps_base + (j + 1) * GPQ))
                    load_q(j + 2)
                    rf = rcp_ring.acq()
                    rv = INC("rc")
                    cv = INC("cn")
                    if on("vector"):
                        W("pv", pv_base + (j + 1) * GPQ)
                        mkw(rf)
                        eng.reciprocal(rcpb[j % 2][:, :], PS[j % 2][:, :]
                                       ).then_inc(sems["rc"], 1)
                        eng.drain()
                        eng.tensor_tensor(arena[:, h * T + qg * TGW: h * T + qg * TGW + TGW],
                                          PC[j % 2][:, :], rcpb[j % 2][:, :], ALU.mult
                                          ).then_inc(sems["cn"], 1)
                    lastfree[f"PS{j % 2}"] = ("rc", rv)
                    lastfree[f"PC{j % 2}"] = ("cn", cv)
                    rcp_ring.rel(("cn", cv))
                emit_pv(NG - 2)
                emit_pv(NG - 1)
                for i in range(4):
                    lastfree[f"P{i}"] = ("ax", ctr["ax"])
                pvtot = ("pv", ctr["pv"])
                stg_ring.rel(pvtot)
                stg_ring.rel(pvtot)
                for _ in range(4):
                    xs_ring.rel(pvtot)
                return ("cn", ctr["cn"])

            lvl = PHASES
            pre1 = [slab_load(qwT, H, 0, True, 0),
                    slab_load(qwT, H, SLABW, True, SLABW)]
            ln1 = ln_phase(xT, None)
            if lvl >= 2:
                proj_phase("ffn1", qwT, 0, ln1, preload=pre1)
                if on("gpsimd"):
                    mkw(store_totals())
            if lvl >= 3:
                preq = [qkv_load(0), qkv_load(1)]
                ln2 = ln_phase(r2d, ("mm", ctr["mm"]))
            if lvl >= 4:
                qkv_phase(ln2, preq)
            if lvl >= 6:
                attn = attention()
                state["buffree"][0] = ("pv", ctr["pv"])
                state["buffree"][1] = ("pv", ctr["pv"])
            if lvl >= 7:
                proj_phase("outp", wo, 0, attn)
                if on("gpsimd"):
                    mkw(store_totals())
            if lvl >= 8:
                pre2 = [slab_load_bf16(wffd, 0), slab_load_bf16(wffd, SLABW)]
                ln3 = ln_phase(r3d, ("mm", ctr["mm"]))
            if lvl >= 9:
                proj_phase("ffn2", wffd, 0, ln3, preload=pre2)
                if on("gpsimd"):
                    mkw(store_totals())

            if DEV_TAP is not None:
                if DEV_TAP == "arena":
                    for blk in range(NI):
                        for part in range(4):
                            v = INC("ev")
                            if on("vector"):
                                W("cl", ctr["cl"])
                                eng.tensor_copy(xstg[0][:, :],
                                                arena[:, blk * T + part * TGW: blk * T + (part + 1) * TGW]
                                                ).then_inc(sems["ev"], 1)
                            c = INC("cl", 16)
                            if on("gpsimd"):
                                W("ev", v)
                                eng.dma_start(out=bass.AP(out, blk * 128 * T + part * TGW,
                                                          [[T, 128], [1, TGW]]),
                                              in_=xstg[0][:, :]).then_inc(sems["cl"], 16)
                    if on("gpsimd"):
                        W("cl", ctr["cl"])
                else:
                    srcs = {"r2d": (r2d, [[T, H], [1, T]], 0),
                            "r3d": (r3d, [[T, H], [1, T]], 0),
                            "qscr": (qscr, [[1, HEADS * 128 * T]], 1),
                            "kvm0": (kvm[0], [[1, 2 * KVG]], 1),
                            "kvm1": (kvm[1], [[1, 2 * KVG]], 1),
                            "kva0": (kva[0], [[2 * KVG, 2], [1, 2 * KVG]], 1),
                            "kva1": (kva[1], [[2 * KVG, 2], [1, 2 * KVG]], 1),
                            "wffd": (wffd, [[H, H], [1, H]], 1)}
                    t, ap, isbf = srcs[DEV_TAP]
                    c = INC("cl", 16)
                    if on("gpsimd"):
                        mkw(store_totals())
                        if ctr["cc"] > 0:
                            W("cc", ctr["cc"])
                        src_ap = bass.AP(t, 0, ap)
                        n = 1
                        for st_, c_ in ap:
                            n *= c_
                        if isbf:
                            src_ap = src_ap.bitcast(F32)
                            n //= 2
                        eng.dma_start(out=bass.AP(out, 0, [[1, n]]), in_=src_ap
                                      ).then_inc(sems["cl"], 16)
                        W("cl", ctr["cl"])

        @block.sync
        def _(e):
            emit("sync", e)

        @block.gpsimd
        def _(e):
            emit("gpsimd", e)

        @block.vector
        def _(e):
            emit("vector", e)

        @block.scalar
        def _(e):
            emit("scalar", e)

        @block.tensor
        def _(e):
            emit("tensor", e)

    return nc


_NC_CACHE = None


def kernel(L0_attn_x, L0_attn_attn_qkvw, L0_attn_attn_qkvb, L0_attn_attn_ow,
           L0_attn_attn_ob, L0_ffn_qweight, L0_ffn_scale, L0_ffn_zero_point,
           L0_ffn_bias):
    global _NC_CACHE
    x = np.asarray(L0_attn_x, dtype=np.float32)
    wqkv = np.ascontiguousarray(np.asarray(L0_attn_attn_qkvw, dtype=np.float32))
    qkvb = np.asarray(L0_attn_attn_qkvb, dtype=np.float32)
    wo = np.ascontiguousarray(np.asarray(L0_attn_attn_ow, dtype=np.float32))
    ob = np.asarray(L0_attn_attn_ob, dtype=np.float32)
    qw = np.asarray(L0_ffn_qweight, dtype=np.int32)
    fb = np.asarray(L0_ffn_bias, dtype=np.float32)

    qwT = np.ascontiguousarray(qw.T)
    qkvb_cols = np.ascontiguousarray(qkvb.reshape(96, 128).T)
    ob_cols = np.ascontiguousarray(ob.reshape(32, 128).T)
    fb_cols = np.ascontiguousarray(fb.reshape(32, 128).T)
    sczp_cols = np.empty((128, 2), dtype=np.float32)
    sczp_cols[:, 0] = np.float32(L0_ffn_scale)
    sczp_cols[:, 1] = np.float32(L0_ffn_zero_point)
    vb_row = np.ascontiguousarray(qkvb[2 * H:3 * H].reshape(1, H))

    if _NC_CACHE is None:
        _NC_CACHE = _build()
    nc = _NC_CACHE

    in_maps = []
    for c in range(8):
        b, half = c // 2, c % 2
        xTc = np.ascontiguousarray(x[b, half * T:(half + 1) * T, :].T)
        in_maps.append({
            "xT": xTc, "wqkv": wqkv, "wo": wo, "qwT": qwT,
            "qkvb_cols": qkvb_cols, "ob_cols": ob_cols, "fb_cols": fb_cols,
            "sczp_cols": sczp_cols, "vb_row": vb_row,
        })
    res = run_bass_kernel_spmd(nc, in_maps, core_ids=list(range(8)))
    full = np.empty((4, 4096, 4096), dtype=np.float32)
    for c in range(8):
        b, half = c // 2, c % 2
        full[b, half * T:(half + 1) * T, :] = res.results[c]["out"].T
    return full


# version 20 (final)
# speedup vs baseline: 1.0842x; 1.0465x over previous
import numpy as np
from contextlib import ExitStack
import concourse.bass as bass
import concourse.mybir as mybir
from concourse.bass_utils import run_bass_kernel_spmd

F32 = mybir.dt.float32
BF16 = mybir.dt.bfloat16
I32 = mybir.dt.int32
AF = mybir.ActivationFunctionType
ALU = mybir.AluOpType

H = 4096
T = 2048
NI = 32
NTG = 4
TGW = 512
SLABW = 256
STGW = 512
ICB = STGW // SLABW
HEADS = 32
SCALE = 1.0 / float(np.sqrt(128.0))
CSH = 10.0
EPS = 1e-5
KVG = 16 * 128 * T
VOFF = KVG
PHASES = 9
SCRATCH_ON = True
DEV_SLABS = 16
DEV_TAP = None

DMA_SEMS = (["cl", "lb"] + [f"xl{i}" for i in range(4)]
            + [f"wg{i}" for i in range(4)] + [f"wsl{i}" for i in range(2)]
            + [f"wscr{i}" for i in range(2)] + [f"so{i}" for i in range(2)]
            + [f"sk{i}" for i in range(2)] + [f"sv{i}" for i in range(2)]
            + [f"vb{i}" for i in range(2)] + [f"qt{i}" for i in range(2)]
            + [f"bc{i}" for i in range(2)])
ENG_SEMS = ("wsc", "wc", "wf", "mm", "ea", "ev", "lv", "lsq", "lp", "lr",
            "la", "lr2", "lap", "ps", "ax", "pv", "rc", "cn", "cc", "ini")


def _build():
    nc = bass.Bass("TRN2", target_bir_lowering=False)

    xT = nc.dram_tensor("xT", [H, T], F32, kind="ExternalInput")
    wqkv = nc.dram_tensor("wqkv", [H, 3 * H], F32, kind="ExternalInput")
    wo = nc.dram_tensor("wo", [H, H], F32, kind="ExternalInput")
    qwT = nc.dram_tensor("qwT", [H, H], I32, kind="ExternalInput")
    qkvb_cols = nc.dram_tensor("qkvb_cols", [128, 96], F32, kind="ExternalInput")
    ob_cols = nc.dram_tensor("ob_cols", [128, 32], F32, kind="ExternalInput")
    fb_cols = nc.dram_tensor("fb_cols", [128, 32], F32, kind="ExternalInput")
    sczp_cols = nc.dram_tensor("sczp_cols", [128, 2], F32, kind="ExternalInput")
    vb_row = nc.dram_tensor("vb_row", [1, H], F32, kind="ExternalInput")
    out = nc.dram_tensor("out", [H, T], F32, kind="ExternalOutput")

    r2d = nc.dram_tensor("r2d", [H, T], F32)
    r3d = nc.dram_tensor("r3d", [H, T], F32)
    wffd = nc.dram_tensor("wffd", [H, H], BF16)
    qscr = nc.dram_tensor("qscr", [HEADS * 128 * T], BF16)
    kvm = [nc.dram_tensor(f"kvm{g}", [2 * KVG], BF16) for g in range(2)]
    kva = [nc.dram_tensor(f"kva{g}", [2, 2 * KVG], BF16) for g in range(2)]
    lnd = nc.dram_tensor("lnd", [4, TGW], BF16)
    rcpd = nc.dram_tensor("rcpd", [2, TGW], F32)

    with ExitStack() as stk:
        E = stk.enter_context
        arena = E(nc.sbuf_tensor("arena", [128, NI * T], BF16))
        wslab = [E(nc.sbuf_tensor(f"wslab{i}", [128, NI * SLABW], BF16)) for i in range(2)]
        wstg_f = [E(nc.sbuf_tensor(f"wstgf{i}", [128, STGW], F32)) for i in range(4)]
        wstg_i = [w[:, :].bitcast(I32) for w in wstg_f]
        xstg = [E(nc.sbuf_tensor(f"xstg{i}", [128, TGW], F32)) for i in range(4)]
        r2b = [E(nc.sbuf_tensor(f"r2b{i}", [128, TGW], F32)) for i in range(4)]
        qkb = [E(nc.sbuf_tensor(f"qkb{i}", [128, TGW], BF16)) for i in range(4)]
        vev = [E(nc.sbuf_tensor(f"vev{i}", [128, SLABW], BF16)) for i in range(4)]
        vbc = [E(nc.sbuf_tensor(f"vbc{i}", [128, SLABW], F32)) for i in range(2)]
        sqb = [E(nc.sbuf_tensor(f"sqb{i}", [128, TGW], BF16)) for i in range(4)]
        abc = E(nc.sbuf_tensor("abc0", [128, TGW], BF16))
        bbc = E(nc.sbuf_tensor("bbc0", [128, TGW], BF16))
        qtb = [E(nc.sbuf_tensor(f"qtb{i}", [128, TGW], BF16)) for i in range(2)]
        prb = [E(nc.sbuf_tensor(f"prb{i}", [128, TGW], BF16)) for i in range(2)]
        rcpb = [E(nc.sbuf_tensor(f"rcpb{i}", [128, TGW], F32)) for i in range(2)]
        onesc = E(nc.sbuf_tensor("onesc", [128, 1], BF16))
        onesf = E(nc.sbuf_tensor("onesf", [128, 128], BF16))
        eps128 = E(nc.sbuf_tensor("eps128", [128, 1], F32))
        negc = E(nc.sbuf_tensor("negc", [128, 1], F32))
        epsc = E(nc.sbuf_tensor("epsc", [1, 1], F32))
        zeroc = E(nc.sbuf_tensor("zeroc", [128, 1], F32))
        qkvb_sb = E(nc.sbuf_tensor("qkvb_sb", [128, 96], F32))
        ob_sb = E(nc.sbuf_tensor("ob_sb", [128, 32], F32))
        fb_sb = E(nc.sbuf_tensor("fb_sb", [128, 32], F32))
        fbh_sb = E(nc.sbuf_tensor("fbh_sb", [128, 32], F32))
        sczp_sb = E(nc.sbuf_tensor("sczp_sb", [128, 2], F32))
        schalf = E(nc.sbuf_tensor("schalf", [128, 1], F32))

        SCR = [E(nc.psum_tensor(f"scr{i}", [128, 1024], F32)) for i in range(2)]
        P = [SCR[i // 2][:, (i % 2) * 512:(i % 2) * 512 + 512] for i in range(4)]
        PS = [E(nc.psum_tensor(f"pss{i}", [128, 512], F32)) for i in range(2)]
        PC = [E(nc.psum_tensor(f"pcc{i}", [128, 512], F32)) for i in range(2)]

        sems = {}
        for s in list(DMA_SEMS) + list(ENG_SEMS):
            sems[s] = E(nc.semaphore(s))
        block = E(nc.Block())

        def emit(ME, eng):
            ctr = dict((s, 0) for s in sems)
            lastfree = {}

            def on(name):
                return ME == name

            def W(sem, val):
                if val > 0:
                    eng.wait_ge(sems[sem], val)

            def INC(sem, n=1):
                ctr[sem] += n
                return ctr[sem]

            def mkw(rec):
                if rec is None:
                    return
                if isinstance(rec, list):
                    for r in rec:
                        mkw(r)
                    return
                if rec[1] > 0:
                    eng.wait_ge(sems[rec[0]], rec[1])

            def DMA(engine, sem, out_ap, in_ap, waits=None):
                val = INC(sem, 16)
                if on(engine):
                    mkw(waits)
                    eng.dma_start(out=out_ap, in_=in_ap).then_inc(sems[sem], 16)
                return (sem, val)

            class Ring:
                def __init__(self, n):
                    self.n = n
                    self.h = []
                def acq(self):
                    return self.h[-self.n] if len(self.h) >= self.n else None
                def rel(self, rec):
                    self.h.append(rec)

            DMA("sync", "cl", qkvb_sb[:, :], qkvb_cols[:, :])
            DMA("sync", "cl", ob_sb[:, :], ob_cols[:, :])
            DMA("sync", "cl", fb_sb[:, :], fb_cols[:, :])
            consts_w = DMA("sync", "cl", sczp_sb[:, :], sczp_cols[:, :])
            if on("vector"):
                eng.memset(onesc[:, :], 1.0)
                eng.memset(onesf[:, :], 1.0)
                eng.memset(eps128[:, :], EPS)
                eng.memset(negc[:, :], -CSH)
                eng.memset(epsc[:, :], EPS)
                eng.memset(zeroc[:, :], 0.0).then_inc(sems["ini"], 1)
                mkw(consts_w)
                eng.tensor_scalar_mul(schalf[:, :], sczp_sb[:, 0:1], 0.5)
                eng.tensor_scalar_mul(fbh_sb[:, :], fb_sb[:, :], 0.5).then_inc(sems["ini"], 1)
            INC("ini", 2)
            ini_w = ("ini", 1)
            ini2_w = ("ini", 2)

            xs_ring = Ring(4)
            r2_ring = Ring(4)
            qk_ring = Ring(4)
            vev_ring = Ring(4)
            vbc_ring = Ring(2)
            sq_ring = Ring(4)
            ab_ring = Ring(1)
            stg_ring = Ring(4)
            slab_ring = Ring(2)
            qt_ring = Ring(2)
            rcp_ring = Ring(2)

            state = dict(gslab=0, pg=0, jat=0, prj=0, xs=0, qdone={},
                         buffree={}, vbcfree={})

            def xload(src, offset):
                s = state["xs"] % 4
                state["xs"] += 1
                rec = DMA("gpsimd", f"xl{s}", xstg[s][:, :],
                          bass.AP(src, offset, [[T, 128], [1, TGW]]),
                          waits=xs_ring.acq())
                return s, rec

            def ln_phase(src, arena_free):
                so_tot = [("so0", ctr["so0"]), ("so1", ctr["so1"])]
                sk_tot = [("sk0", ctr["sk0"]), ("sk1", ctr["sk1"])]
                for tg in range(NTG):
                    SA, SB = (PS[0], PS[1]) if tg % 2 == 0 else (PC[0], PC[1])
                    ka, kb2 = ("PS0", "PS1") if tg % 2 == 0 else ("PC0", "PC1")
                    for ic in range(NI):
                        acol = ic * T + tg * TGW
                        ki, krec = xload(src, ic * 128 * T + tg * TGW)
                        sf = sq_ring.acq()
                        v = INC("lv")
                        q = INC("lsq")
                        if on("scalar"):
                            mkw(krec)
                            if tg == 0 and ic == 0:
                                mkw(arena_free)
                                mkw(ini_w)
                            eng.activation(arena[:, acol:acol + TGW], xstg[ki][:, :],
                                           AF.Identity, bias=zeroc[:, :],
                                           scale=1.0).then_inc(sems["lv"], 1)
                            mkw(sf)
                            eng.activation(sqb[(q - 1) % 4][:, :], xstg[ki][:, :], AF.Square,
                                           bias=zeroc[:, :], scale=1.0).then_inc(sems["lsq"], 1)
                        xs_ring.rel([("lv", v), ("lsq", q)])
                        p = INC("lp")
                        if on("tensor"):
                            W("lv", v)
                            W("lsq", q)
                            if ic == 0:
                                mkw(ini_w)
                                mkw(lastfree.get(ka))
                                mkw(lastfree.get(kb2))
                            eng.matmul(SA[:, :], onesf[:, :], arena[:, acol:acol + TGW],
                                       start=(ic == 0), stop=(ic == NI - 1))
                            eng.matmul(SB[:, :], onesf[:, :], sqb[(q - 1) % 4][:, :],
                                       start=(ic == 0), stop=(ic == NI - 1)).then_inc(sems["lp"], 1)
                        sq_ring.rel(("lp", p))
                    r = INC("lr")
                    if on("vector"):
                        W("lp", ctr["lp"])
                        if tg == 0:
                            mkw(so_tot)
                        eng.tensor_scalar_mul(rcpb[0][:, :], SA[:, :], 1.0 / 4096.0)
                        eng.tensor_scalar_mul(rcpb[1][:, :], SB[:, :], 1.0 / 4096.0)
                        eng.drain()
                        eng.tensor_mul(r2b[0][:, :], rcpb[0][:, :], rcpb[0][:, :])
                        eng.drain()
                        eng.tensor_sub(rcpb[1][:, :], rcpb[1][:, :],
                                       r2b[0][:, :]).then_inc(sems["lr"], 1)
                    lastfree[ka] = ("lr", r)
                    lastfree[kb2] = ("lr", r)
                    a = INC("la")
                    if on("scalar"):
                        W("lr", r)
                        eng.activation(r2b[1][:, :], rcpb[1][:, :], AF.Sqrt,
                                       bias=eps128[:, :], scale=1.0).then_inc(sems["la"], 1)
                    r2v = INC("lr2")
                    ap = INC("lap")
                    if on("vector"):
                        W("la", a)
                        eng.reciprocal(r2b[0][:, :], r2b[1][:, :])
                        eng.drain()
                        eng.tensor_copy(abc[:, :], r2b[0][:, :])
                        eng.tensor_mul(rcpb[1][:, :], rcpb[0][:, :], r2b[0][:, :])
                        eng.drain()
                        eng.tensor_scalar_mul(bbc[:, :], rcpb[1][:, :],
                                              -1.0).then_inc(sems["lr2"], 1)
                        if tg == 0:
                            mkw(sk_tot)
                        tq = [prb[0], prb[1], qkb[0], qkb[1]]
                        for ic in range(0, NI, 4):
                            cols = [(ic + i) * T + tg * TGW for i in range(4)]
                            for i in range(4):
                                eng.tensor_mul(tq[i][:, :], arena[:, cols[i]:cols[i] + TGW], abc[:, :])
                            eng.drain()
                            for i in range(4):
                                ins = eng.tensor_add(arena[:, cols[i]:cols[i] + TGW], tq[i][:, :], bbc[:, :])
                            eng.drain()
                        ins.then_inc(sems["lap"], 1)
                return ("lap", ctr["lap"])

            def slab_load(wt, ncols, c0, is_int, to_scratch):
                gs = state["gslab"]
                state["gslab"] += 1
                sbuf = wslab[gs % 2]
                fr = state["buffree"].get(gs % 2)
                for j in range(NI * SLABW // STGW):
                    sf = stg_ring.acq()
                    si = len(stg_ring.h) % 4
                    krec = DMA("sync", f"wg{si}",
                               (wstg_i if is_int else wstg_f)[si][:, :],
                               bass.AP(wt, (j * ICB * 128) * ncols + c0,
                                       [[ncols, 128], [128 * ncols, ICB], [1, SLABW]]),
                               waits=sf)
                    v = INC("wsc")
                    if on("vector"):
                        mkw(krec)
                        if j == 0:
                            mkw(fr)
                            if to_scratch is not None and SCRATCH_ON and gs >= 2:
                                mkw(("wscr%d" % (gs % 2), 16 * (gs // 2)))
                        dst = sbuf[:, j * STGW:(j + 1) * STGW]
                        if is_int:
                            ins = eng.tensor_scalar(dst, wstg_i[si][:, :],
                                                    sczp_sb[:, 1:2], schalf[:, 0:1],
                                                    ALU.subtract, ALU.mult)
                        else:
                            ins = eng.tensor_copy(dst, wstg_f[si][:, :])
                        ins.then_inc(sems["wsc"], 1)
                    stg_ring.rel(("wsc", v))
                ready = ("wsc", ctr["wsc"])
                if to_scratch is not None and SCRATCH_ON:
                    DMA("sync", f"wscr{gs % 2}",
                        bass.AP(wffd, to_scratch, [[H, 128], [128 * H, NI], [1, SLABW]]),
                        sbuf[:, :], waits=ready)
                return ready, gs

            def slab_load_bf16(wt, c0):
                gs = state["gslab"]
                state["gslab"] += 1
                sbuf = wslab[gs % 2]
                ready = DMA("sync", f"wsl{gs % 2}", sbuf[:, :],
                            bass.AP(wt, c0, [[H, 128], [128 * H, NI], [1, SLABW]]),
                            waits=[state["buffree"].get(gs % 2),
                                   ("wscr0", ctr["wscr0"]), ("wscr1", ctr["wscr1"])])
                return ready, gs

            def slab_done(gs):
                state["buffree"][gs % 2] = ("mm", ctr["mm"])

            def proj_phase(kind, wt, cbase, arena_ready, preload=None):
                nsl = DEV_SLABS if kind == "ffn1" else 16

                def load(sl):
                    c0 = cbase + sl * SLABW
                    if kind == "ffn2":
                        return slab_load_bf16(wt, c0)
                    scr = c0 if kind == "ffn1" else None
                    return slab_load(wt, H, c0, kind == "ffn1", scr)

                pend = preload[0] if preload else load(0)
                for sl in range(nsl):
                    c0 = cbase + sl * SLABW
                    if sl + 1 >= nsl:
                        nxt = None
                    elif preload and sl == 0:
                        nxt = preload[1]
                    else:
                        nxt = load(sl + 1)
                    ready, gs = pend
                    for otl in range(2):
                        oc = c0 + otl * 128
                        ot = oc // 128
                        for tg in range(NTG):
                            pg = state["pg"]
                            state["pg"] += 1
                            bank = P[pg % 4]
                            bn = f"P{pg % 4}"
                            g = INC("mm")
                            if on("tensor"):
                                if otl == 0 and tg == 0:
                                    mkw(ready)
                                    mkw(arena_ready)
                                mkw(lastfree.get(bn))
                                for ic in range(NI):
                                    ins = eng.matmul(
                                        bank[:, :],
                                        wslab[gs % 2][:, ic * SLABW + otl * 128: ic * SLABW + otl * 128 + 128],
                                        arena[:, ic * T + tg * TGW: ic * T + tg * TGW + TGW],
                                        start=(ic == 0), stop=(ic == NI - 1),
                                    )
                                ins.then_inc(sems["mm"], 1)
                            slot = state["prj"] % 4
                            state["prj"] += 1
                            if kind == "outp":
                                biascol = ob_sb[:, ot:ot + 1]
                            else:
                                biascol = fbh_sb[:, ot:ot + 1]
                            rfr = r2_ring.acq()
                            a = INC("ea")
                            if on("scalar"):
                                W("mm", g)
                                mkw(rfr)
                                eng.activation(r2b[slot][:, :], bank[:, :], AF.Identity,
                                               bias=biascol, scale=1.0).then_inc(sems["ea"], 1)
                            lastfree[bn] = ("ea", a)
                            rsrc = {"ffn1": xT, "outp": r2d, "ffn2": r3d}[kind]
                            xi, xrec = xload(rsrc, ot * 128 * T + tg * TGW)
                            v = INC("ev")
                            if on("vector"):
                                W("ea", a)
                                mkw(xrec)
                                eng.tensor_add(r2b[slot][:, :], r2b[slot][:, :],
                                               xstg[xi][:, :]).then_inc(sems["ev"], 1)
                            xs_ring.rel(("ev", v))
                            dstt = {"ffn1": r2d, "outp": r3d, "ffn2": out}[kind]
                            srec = DMA("gpsimd", f"so{slot % 2}",
                                       bass.AP(dstt, ot * 128 * T + tg * TGW, [[T, 128], [1, TGW]]),
                                       r2b[slot][:, :], waits=("ev", v))
                            r2_ring.rel(srec)
                    slab_done(gs)
                    pend = nxt

            def qk_tiles(c0, ready, gs, arena_ready):
                    for otl in range(2):
                        oc = c0 + otl * 128
                        for tg in range(NTG):
                            pg = state["pg"]
                            state["pg"] += 1
                            bank = P[pg % 4]
                            bn = f"P{pg % 4}"
                            g = INC("mm")
                            if on("tensor"):
                                if otl == 0 and tg == 0:
                                    mkw(ready)
                                    mkw(arena_ready)
                                mkw(lastfree.get(bn))
                                for ic in range(NI):
                                    ins = eng.matmul(
                                        bank[:, :],
                                        wslab[gs % 2][:, ic * SLABW + otl * 128: ic * SLABW + otl * 128 + 128],
                                        arena[:, ic * T + tg * TGW: ic * T + tg * TGW + TGW],
                                        start=(ic == 0), stop=(ic == NI - 1),
                                    )
                                ins.then_inc(sems["mm"], 1)
                            efr = qk_ring.acq()
                            slot = len(qk_ring.h) % 4
                            a = INC("ea")
                            if on("scalar"):
                                W("mm", g)
                                mkw(efr)
                                eng.activation(qkb[slot][:, :], bank[:, :], AF.Identity,
                                               bias=qkvb_sb[:, oc // 128: oc // 128 + 1],
                                               scale=1.0).then_inc(sems["ea"], 1)
                            lastfree[bn] = ("ea", a)
                            hid = (oc // 128) % 32
                            if oc < H:
                                dst = bass.AP(qscr, hid * 128 * T + tg * TGW, [[T, 128], [1, TGW]])
                            else:
                                dst = bass.AP(kvm[hid // 16], (hid % 16) * 128 * T + tg * TGW,
                                              [[T, 128], [1, TGW]])
                            srec = DMA("scalar", f"sk{slot % 2}", dst, qkb[slot][:, :],
                                       waits=("ea", a))
                            qk_ring.rel(srec)
                            if oc < H and tg == NTG - 1:
                                state["qdone"][hid] = srec

            def v_tiles(sv, ready, gs, vrec, arena_ready):
                    grp = sv // 8
                    lastv = None
                    for ts in range(16):
                        pg = state["pg"]
                        state["pg"] += 1
                        bank = P[pg % 4]
                        bn = f"P{pg % 4}"
                        g = INC("mm")
                        if on("tensor"):
                            if ts == 0:
                                mkw(ready)
                                mkw(arena_ready)
                            mkw(lastfree.get(bn))
                            for ic in range(NI):
                                ins = eng.matmul(
                                    bank[:, :SLABW],
                                    arena[:, ic * T + ts * 128: ic * T + ts * 128 + 128],
                                    wslab[gs % 2][:, ic * SLABW: (ic + 1) * SLABW],
                                    start=(ic == 0), stop=(ic == NI - 1),
                                )
                            ins.then_inc(sems["mm"], 1)
                        vfr2 = vev_ring.acq()
                        slot = len(vev_ring.h) % 4
                        v = INC("ev")
                        if on("vector"):
                            W("mm", g)
                            mkw(vrec)
                            mkw(vfr2)
                            eng.tensor_add(vev[slot][:, :], bank[:, :SLABW],
                                           vbc[sv % 2][:, :]).then_inc(sems["ev"], 1)
                        lastfree[bn] = ("ev", v)
                        lastv = ("ev", v)
                        h0 = (sv * SLABW) // 128
                        srec = DMA("scalar", f"sv{slot % 2}",
                                   bass.AP(kvm[grp], VOFF + (h0 % 16) * T * 128 + ts * 128 * 128,
                                           [[128, 128], [T * 128, 2], [1, 128]]),
                                   vev[slot][:, :], waits=("ev", v))
                        vev_ring.rel(srec)
                    state["vbcfree"][sv % 2] = lastv

            def store_totals():
                return [(s, ctr[s]) for s in ("so0", "so1", "sk0", "sk1", "sv0", "sv1")
                        if ctr[s] > 0]

            QKV_SLABS = ([("k", H + i * SLABW) for i in range(8)]
                         + [("v", i) for i in range(8)]
                         + [("k", H + (8 + i) * SLABW) for i in range(8)]
                         + [("v", 8 + i) for i in range(8)]
                         + [("k", i * SLABW) for i in range(16)])

            def qkv_load(idx):
                if idx >= len(QKV_SLABS):
                    return None
                kind, a = QKV_SLABS[idx]
                if kind == "v":
                    c0 = 2 * H + a * SLABW
                    r, g = slab_load(wqkv, 3 * H, c0, False, None)
                    vrec = DMA("sync", f"vb{a % 2}", vbc[a % 2][:, :],
                               bass.AP(vb_row, a * SLABW, [[0, 128], [1, SLABW]]),
                               waits=state["vbcfree"].get(a % 2))
                    return r, g, vrec
                r, g = slab_load(wqkv, 3 * H, a, False, None)
                return r, g, None

            def emit_collective(grp):
                INC("cc")
                if on("gpsimd"):
                    mkw(store_totals())
                    eng.collective_compute(
                        "AllGather", ALU.bypass,
                        replica_groups=[[0, 1], [2, 3], [4, 5], [6, 7]],
                        ins=[kvm[grp].ap().opt()], outs=[kva[grp].ap().opt()],
                    ).then_inc(sems["cc"], 1)

            def qkv_phase(arena_ready, preload):
                pend = preload[0] if preload else qkv_load(0)
                for idx in range(len(QKV_SLABS)):
                    if preload and idx == 0:
                        nxt = preload[1]
                    else:
                        nxt = qkv_load(idx + 1)
                    kind, a = QKV_SLABS[idx]
                    ready, gs, vrec = pend
                    if kind == "v":
                        v_tiles(a, ready, gs, vrec, arena_ready)
                    else:
                        qk_tiles(a, ready, gs, arena_ready)
                    slab_done(gs)
                    pend = nxt
                    if idx == 15:
                        emit_collective(0)
                    elif idx == 31:
                        emit_collective(1)

            def attention():
                kvload = {}
                qload = {}
                mm_tot = ctr["mm"]
                ps_base, ax_base, pv_base = ctr["ps"], ctr["ax"], ctr["pv"]
                start_tot = [("wsc", ctr["wsc"]), ("lv", ctr["lv"]),
                             ("lsq", ctr["lsq"]), ("ev", ctr["ev"])]
                prb4 = [wstg_f[i][:, :].bitcast(BF16) for i in range(4)]
                NQG = HEADS * NTG
                GPQ = 16
                NG = NQG * GPQ

                def load_kv(h):
                    grp = h // 16
                    ws = f"wsl{h % 2}"
                    first_waits = [("cc", grp + 1)]
                    if h < 2:
                        first_waits.append(("mm", mm_tot))
                    else:
                        first_waits.append(("pv", pv_base + (h - 1) * NTG * GPQ))
                    recs = None
                    for half2 in range(2):
                        DMA("gpsimd", ws,
                            wslab[h % 2][:, half2 * T: half2 * T + T],
                            bass.AP(kva[grp], half2 * 2 * KVG + (h % 16) * 128 * T,
                                    [[T, 128], [1, T]]),
                            waits=first_waits if half2 == 0 else None)
                        recs = DMA("gpsimd", ws,
                                   wslab[h % 2][:, H + half2 * T: H + half2 * T + T],
                                   bass.AP(kva[grp], half2 * 2 * KVG + VOFF + (h % 16) * T * 128,
                                           [[128, 128], [16384, 16], [1, 128]]))
                    kvload[h] = recs

                def load_q(j):
                    if j >= NQG:
                        return
                    hq, qgq = j // NTG, j % NTG
                    w = [state["qdone"][hq]]
                    if j >= 2:
                        w.append(("ps", ps_base + (j - 1) * GPQ))
                    qload[j] = DMA("sync", f"qt{j % 2}", qtb[j % 2][:, :],
                                   bass.AP(qscr, hq * 128 * T + qgq * TGW,
                                           [[T, 128], [1, TGW]]),
                                   waits=w)

                def emit_ps(g, extraw):
                    j2, h2 = g // GPQ, g // (NTG * GPQ)
                    INC("ps")
                    if on("tensor"):
                        mkw(extraw)
                        for half in range(2):
                            kc = 2 * (g % GPQ) + half
                            ins = eng.matmul(
                                SCR[g % 2][:, half * 512:half * 512 + 512],
                                wslab[h2 % 2][:, kc * 128:(kc + 1) * 128],
                                qtb[j2 % 2][:, :], start=True, stop=True)
                        ins.then_inc(sems["ps"], 1)

                def emit_pv(g):
                    j2, kb2, h2 = g // GPQ, g % GPQ, g // (NTG * GPQ)
                    INC("pv")
                    if on("tensor"):
                        if kb2 == 0:
                            mkw([lastfree.get(f"PC{j2 % 2}"),
                                 lastfree.get(f"PS{j2 % 2}")])
                        for half in range(2):
                            kc = 2 * kb2 + half
                            st = (kb2 == 0 and half == 0)
                            sp = (kb2 == GPQ - 1 and half == 1)
                            eng.matmul(PC[j2 % 2][:, :],
                                       wslab[h2 % 2][:, H + kc * 128: H + kc * 128 + 128],
                                       prb4[g % 4][:, half * 512:half * 512 + 512],
                                       start=st, stop=sp)
                            ins = eng.matmul(PS[j2 % 2][:, :], onesf[:, :],
                                             prb4[g % 4][:, half * 512:half * 512 + 512],
                                             start=st, stop=sp)
                        ins.then_inc(sems["pv"], 1)

                def emit_ax(g):
                    INC("ax")
                    if on("scalar"):
                        w = [("ps", ps_base + g + 1)]
                        if g >= 4:
                            w.append(("pv", pv_base + g - 3))
                        else:
                            w += start_tot
                        mkw(w)
                        eng.activation(prb4[g % 4], SCR[g % 2][:, :], AF.Exp,
                                       bias=negc[:, :], scale=SCALE
                                       ).then_inc(sems["ax"], 1)

                load_q(0)
                load_q(1)
                for j in range(NQG):
                    h, qg = j // NTG, j % NTG
                    if qg == 0:
                        if h == 0:
                            load_kv(0)
                            load_kv(1)
                        elif h < HEADS - 1:
                            load_kv(h + 1)
                    for sb in range(GPQ // 2):
                        g0 = j * GPQ + 2 * sb
                        w = []
                        if sb == 0:
                            w += [qload[j], kvload[h]]
                        if g0 >= 2:
                            w.append(("ax", ax_base + g0))
                        emit_ps(g0, w)
                        emit_ps(g0 + 1, None)
                        if g0 >= 2:
                            emit_pv(g0 - 2)
                            emit_pv(g0 - 1)
                        emit_ax(g0)
                        emit_ax(g0 + 1)
                    qt_ring.rel(("ps", ps_base + (j + 1) * GPQ))
                    load_q(j + 2)
                    rf = rcp_ring.acq()
                    rv = INC("rc")
                    cv = INC("cn")
                    if on("vector"):
                        W("pv", pv_base + (j + 1) * GPQ)
                        mkw(rf)
                        eng.reciprocal(rcpb[j % 2][:, :], PS[j % 2][:, :]
                                       ).then_inc(sems["rc"], 1)
                        eng.drain()
                        eng.tensor_tensor(arena[:, h * T + qg * TGW: h * T + qg * TGW + TGW],
                                          PC[j % 2][:, :], rcpb[j % 2][:, :], ALU.mult
                                          ).then_inc(sems["cn"], 1)
                    lastfree[f"PS{j % 2}"] = ("rc", rv)
                    lastfree[f"PC{j % 2}"] = ("cn", cv)
                    rcp_ring.rel(("cn", cv))
                emit_pv(NG - 2)
                emit_pv(NG - 1)
                for i in range(4):
                    lastfree[f"P{i}"] = ("ax", ctr["ax"])
                pvtot = ("pv", ctr["pv"])
                for _ in range(4):
                    stg_ring.rel(pvtot)
                for _ in range(4):
                    xs_ring.rel(pvtot)
                return ("cn", ctr["cn"])

            lvl = PHASES
            pre1 = [slab_load(qwT, H, 0, True, 0),
                    slab_load(qwT, H, SLABW, True, SLABW)]
            ln1 = ln_phase(xT, None)
            if lvl >= 2:
                proj_phase("ffn1", qwT, 0, ln1, preload=pre1)
                if on("gpsimd"):
                    mkw(store_totals())
            if lvl >= 3:
                preq = [qkv_load(0), qkv_load(1)]
                ln2 = ln_phase(r2d, ("mm", ctr["mm"]))
            if lvl >= 4:
                qkv_phase(ln2, preq)
            if lvl >= 6:
                attn = attention()
                state["buffree"][0] = ("pv", ctr["pv"])
                state["buffree"][1] = ("pv", ctr["pv"])
            if lvl >= 7:
                proj_phase("outp", wo, 0, attn)
                if on("gpsimd"):
                    mkw(store_totals())
            if lvl >= 8:
                pre2 = [slab_load_bf16(wffd, 0), slab_load_bf16(wffd, SLABW)]
                ln3 = ln_phase(r3d, ("mm", ctr["mm"]))
            if lvl >= 9:
                proj_phase("ffn2", wffd, 0, ln3, preload=pre2)
                if on("gpsimd"):
                    mkw(store_totals())

            if DEV_TAP is not None:
                if DEV_TAP == "arena":
                    for blk in range(NI):
                        for part in range(4):
                            v = INC("ev")
                            if on("vector"):
                                W("cl", ctr["cl"])
                                eng.tensor_copy(xstg[0][:, :],
                                                arena[:, blk * T + part * TGW: blk * T + (part + 1) * TGW]
                                                ).then_inc(sems["ev"], 1)
                            c = INC("cl", 16)
                            if on("gpsimd"):
                                W("ev", v)
                                eng.dma_start(out=bass.AP(out, blk * 128 * T + part * TGW,
                                                          [[T, 128], [1, TGW]]),
                                              in_=xstg[0][:, :]).then_inc(sems["cl"], 16)
                    if on("gpsimd"):
                        W("cl", ctr["cl"])
                else:
                    srcs = {"r2d": (r2d, [[T, H], [1, T]], 0),
                            "r3d": (r3d, [[T, H], [1, T]], 0),
                            "qscr": (qscr, [[1, HEADS * 128 * T]], 1),
                            "kvm0": (kvm[0], [[1, 2 * KVG]], 1),
                            "kvm1": (kvm[1], [[1, 2 * KVG]], 1),
                            "kva0": (kva[0], [[2 * KVG, 2], [1, 2 * KVG]], 1),
                            "kva1": (kva[1], [[2 * KVG, 2], [1, 2 * KVG]], 1),
                            "wffd": (wffd, [[H, H], [1, H]], 1)}
                    t, ap, isbf = srcs[DEV_TAP]
                    c = INC("cl", 16)
                    if on("gpsimd"):
                        mkw(store_totals())
                        if ctr["cc"] > 0:
                            W("cc", ctr["cc"])
                        src_ap = bass.AP(t, 0, ap)
                        n = 1
                        for st_, c_ in ap:
                            n *= c_
                        if isbf:
                            src_ap = src_ap.bitcast(F32)
                            n //= 2
                        eng.dma_start(out=bass.AP(out, 0, [[1, n]]), in_=src_ap
                                      ).then_inc(sems["cl"], 16)
                        W("cl", ctr["cl"])

        @block.sync
        def _(e):
            emit("sync", e)

        @block.gpsimd
        def _(e):
            emit("gpsimd", e)

        @block.vector
        def _(e):
            emit("vector", e)

        @block.scalar
        def _(e):
            emit("scalar", e)

        @block.tensor
        def _(e):
            emit("tensor", e)

    return nc


_NC_CACHE = None


def kernel(L0_attn_x, L0_attn_attn_qkvw, L0_attn_attn_qkvb, L0_attn_attn_ow,
           L0_attn_attn_ob, L0_ffn_qweight, L0_ffn_scale, L0_ffn_zero_point,
           L0_ffn_bias):
    global _NC_CACHE
    x = np.asarray(L0_attn_x, dtype=np.float32)
    wqkv = np.ascontiguousarray(np.asarray(L0_attn_attn_qkvw, dtype=np.float32))
    qkvb = np.asarray(L0_attn_attn_qkvb, dtype=np.float32)
    wo = np.ascontiguousarray(np.asarray(L0_attn_attn_ow, dtype=np.float32))
    ob = np.asarray(L0_attn_attn_ob, dtype=np.float32)
    qw = np.asarray(L0_ffn_qweight, dtype=np.int32)
    fb = np.asarray(L0_ffn_bias, dtype=np.float32)

    qwT = np.ascontiguousarray(qw.T)
    qkvb_cols = np.ascontiguousarray(qkvb.reshape(96, 128).T)
    ob_cols = np.ascontiguousarray(ob.reshape(32, 128).T)
    fb_cols = np.ascontiguousarray(fb.reshape(32, 128).T)
    sczp_cols = np.empty((128, 2), dtype=np.float32)
    sczp_cols[:, 0] = np.float32(L0_ffn_scale)
    sczp_cols[:, 1] = np.float32(L0_ffn_zero_point)
    vb_row = np.ascontiguousarray(qkvb[2 * H:3 * H].reshape(1, H))

    if _NC_CACHE is None:
        _NC_CACHE = _build()
    nc = _NC_CACHE

    in_maps = []
    for c in range(8):
        b, half = c // 2, c % 2
        xTc = np.ascontiguousarray(x[b, half * T:(half + 1) * T, :].T)
        in_maps.append({
            "xT": xTc, "wqkv": wqkv, "wo": wo, "qwT": qwT,
            "qkvb_cols": qkvb_cols, "ob_cols": ob_cols, "fb_cols": fb_cols,
            "sczp_cols": sczp_cols, "vb_row": vb_row,
        })
    res = run_bass_kernel_spmd(nc, in_maps, core_ids=list(range(8)))
    full = np.empty((4, 4096, 4096), dtype=np.float32)
    for c in range(8):
        b, half = c // 2, c % 2
        full[b, half * T:(half + 1) * T, :] = res.results[c]["out"].T
    return full
